# version 01
# baseline (speedup 1.0000x reference)
import math
import numpy as np
import ml_dtypes
import concourse.bass as bass
import concourse.mybir as mybir
from concourse.bass_utils import run_bass_kernel_spmd
from concourse.expressions import RuntimeValue
from contextlib import ExitStack

F32 = mybir.dt.float32
BF16 = mybir.dt.bfloat16
I32 = mybir.dt.int32
AF = mybir.ActivationFunctionType
ALU = mybir.AluOpType
AX = mybir.AxisListType.X

D = 4096
T = 1024
TH = T + 3
NKD = D // 128
INTER = 8192
NH = 128
HDM = 64
NG = 8
NCH = 8
CONV_DIM = 10240
PROJ = 18560
DT_MIN, DT_MAX = 1e-3, 100.0
EPS = 1e-5
AHD = 128
AG = 8
KV_DIM = 2048
ASCALE = 1.0 / math.sqrt(AHD)
E_EXP = 8
R_LORA = 16
LORA_SCALE = 2.0

ENGS = ("tensor", "vector", "scalar", "gpsimd", "sync")


class Prog:
    def __init__(self, nc):
        self.nc = nc
        self.streams = {e: [] for e in ENGS}
        self.cnt = {}

    def sem(self, name):
        h = self.nc.alloc_semaphore(name=name)
        self.cnt[h.name] = 0
        return h

    def bump(self, sem, k=1):
        self.cnt[sem.name] += k
        return self.cnt[sem.name]

    def val(self, sem):
        return self.cnt[sem.name]

    def em(self, eng, fn):
        self.streams[eng].append(fn)

    def wait(self, eng, sem, v):
        if v > 0:
            self.em(eng, lambda E, s=sem, vv=v: E.wait_ge(s, vv))

    def barrier(self, bar_sem, engs=ENGS):
        tgt = self.bump(bar_sem, len(engs))
        for e in engs:
            def f(E, s=bar_sem, t=tgt):
                E.sem_inc(s, 1)
                E.wait_ge(s, t)
            self.em(e, f)

    def emit(self, block):
        @block.tensor
        def _(E):
            for fn in self.streams["tensor"]:
                fn(E)

        @block.vector
        def _(E):
            for fn in self.streams["vector"]:
                fn(E)

        @block.scalar
        def _(E):
            for fn in self.streams["scalar"]:
                fn(E)

        @block.gpsimd
        def _(E):
            for fn in self.streams["gpsimd"]:
                fn(E)

        @block.sync
        def _(E):
            for fn in self.streams["sync"]:
                fn(E)


def build_program(debug_taps=(), upto="all", ssd_parts=None, nhb=NG, hb_skip=()):
    nc = bass.Bass("TRN2", target_bir_lowering=False, debug=False)
    gctx = ExitStack()
    E_ = gctx.enter_context

    def din(name, shape, dt=F32):
        return nc.dram_tensor(name, list(shape), dt, kind="ExternalInput").ap()

    def dint(name, shape, dt):
        return nc.dram_tensor(name, list(shape), dt, kind="Internal").ap()

    x_in = din("x_fm", (D, T))
    xhalo_in = din("xhalo_fm", (D, 3))
    wt_inproj = din("wt_inproj", (D, PROJ))
    convw_in = din("convw", (128, 80 * 4))
    convb_in = din("convb", (128, 80))
    dtbias_in = din("dtbias_col", (128, 1))
    acol_in = din("a_col", (128, 1))
    drow_in = din("d_row", (1, 128))
    normw_in = din("normw_row", (1, INTER))
    wt_outproj = din("wt_outproj", (INTER, D))
    wq_in = din("wq", (D, D))
    wkv_in = din("wkv", (D, KV_DIM))
    wo_in = din("wo", (D, D))
    bo_in = din("bo_col", (128, NKD))
    wbase_in = din("wt_base", (D, D))
    bbase_in = din("bbase_col", (128, NKD))
    wrout_in = din("wt_router", (D, E_EXP))
    loraA_in = din("loraA_t", (D, 128))
    loraB_in = din("loraB_t", (128, D))
    e16_in = din("e16", (E_EXP, 128))
    ones_row_in = din("ones_row", (1, 128))
    ident_in = din("ident_f", (128, 128))
    tril_in = din("tril_sl", (128, 128))
    amask_in = din("attn_masks", (128, 4 * 512))
    gatecol_in = din("gate_col", (128, 1))
    idx_in = din("idx", (1, 8), I32)

    out_d = nc.dram_tensor("out_fm", [D, T], F32, kind="ExternalOutput").ap()

    r_dram = dint("r_dram", (D, TH), F32)
    xln_dram = dint("xln_dram", (D, TH), BF16)
    bc_dram = dint("bc_dram", (2 * NG * 128, T), BF16)
    gate_dram = dint("gate_dram", (T, INTER), BF16)
    y_dram = dint("y_dram", (T, INTER), BF16)
    yn_dram = dint("yn_dram", (INTER, T), BF16)
    cc1_in = dint("cc1_in", (128, NH * HDM), F32)
    cc1_out = dint("cc1_out", (256, NH * HDM), F32)
    q_cc = dint("q_cc", (D, T), BF16)
    k_cc = dint("k_cc", (AG * 128, T), BF16)
    v_cc = dint("v_cc", (T, AG * 128), BF16)
    q_g = dint("q_g", (2 * D, T), BF16)
    k_g = nc.dram_tensor("k_g", [8 * AG * 128, T], BF16, kind="Internal", addr_space="Shared").ap()
    v_g = nc.dram_tensor("v_g", [8 * T, AG * 128], BF16, kind="Internal", addr_space="Shared").ap()
    ctx_dram = dint("ctx_dram", (16 * 128, 2 * T), BF16)
    cc3_in = dint("cc3_in", (16 * 128, T), BF16)
    cc3_out = dint("cc3_out", (2 * 16 * 128, T), BF16)
    ctx_all = dint("ctx_all", (D, T), BF16)
    q_gg = dint("q_gg", (2 * 2048, T), BF16)
    k_gg = dint("k_gg", (8 * 128, T), BF16)
    v_gg = dint("v_gg", (8 * T, 128), BF16)

    dbg = {}

    def tap(name, ap):
        if name in debug_taps:
            o = nc.dram_tensor("dbg_" + name, list(ap.shape), ap.dtype,
                               kind="ExternalOutput").ap()
            dbg["dbg_" + name] = (o, ap)

    def tap_now(name, ap):
        if name in debug_taps:
            o = nc.dram_tensor("dbg_" + name, list(ap.shape), ap.dtype,
                               kind="ExternalOutput").ap()
            y_dma(o[:, :], ap[:, :], sem=s_st)
            dbg["dbg_" + name] = (o, None)

    ones_row = E_(nc.sbuf_tensor("sb_ones_row", [1, 128], F32))
    ones_col_b = E_(nc.sbuf_tensor("sb_ones_col_b", [128, 1], BF16))
    ident_f = E_(nc.sbuf_tensor("sb_ident_f", [128, 128], F32))
    ident_b = E_(nc.sbuf_tensor("sb_ident_b", [128, 128], BF16))
    tril_b = E_(nc.sbuf_tensor("sb_tril_b", [128, 128], BF16))
    convw = E_(nc.sbuf_tensor("sb_convw", [128, 320], F32))
    convb = E_(nc.sbuf_tensor("sb_convb", [128, 80], F32))
    dtbias = E_(nc.sbuf_tensor("sb_dtbias", [128, 1], F32))
    acol = E_(nc.sbuf_tensor("sb_acol", [128, 1], F32))
    drow = E_(nc.sbuf_tensor("sb_drow", [1, 128], F32))
    normw_row = E_(nc.sbuf_tensor("sb_normw", [1, 1024], F32))
    bbase = E_(nc.sbuf_tensor("sb_bbase", [128, NKD], F32))
    bocol = E_(nc.sbuf_tensor("sb_bocol", [128, NKD], F32))
    e16 = E_(nc.sbuf_tensor("sb_e16", [E_EXP, 128], F32))
    gatecol = E_(nc.sbuf_tensor("sb_gatecol", [128, 1], F32))
    idxs = E_(nc.sbuf_tensor("sb_idxs", [1, 8], I32))
    scrA = E_(nc.sbuf_tensor("scrA", [128, 3104], F32))
    scrB = E_(nc.sbuf_tensor("scrB", [128, 1024], F32))
    scrC = E_(nc.sbuf_tensor("scrC", [128, 4096], BF16))
    scrD = E_(nc.sbuf_tensor("scrD", [128, 3072], BF16))
    evac = E_(nc.sbuf_tensor("evac", [128, 1024], F32))
    rowA = E_(nc.sbuf_tensor("rowA", [1, 192], F32))
    rowB = E_(nc.sbuf_tensor("rowB", [1, 192], F32))
    colA = E_(nc.sbuf_tensor("colA", [128, 4], F32))

    pp = E_(nc.psum_tensor("pp", [128, 4096], F32))
    def bank(i, w=512):
        return pp[:, i * 512: i * 512 + w]
    ppb = pp[:, 7 * 512: 8 * 512].bitcast(BF16)

    P = Prog(nc)
    s_wld = P.sem("s_wld")
    s_xld = P.sem("s_xld")
    s_mm = P.sem("s_mm")
    s_ve = P.sem("s_ve")
    s_ac = P.sem("s_ac")
    s_st = P.sem("s_st")
    s_gst = P.sem("s_gst")
    s_cc = P.sem("s_cc")
    s_gp = P.sem("s_gp")
    s_bar = P.sem("s_bar")

    def g_dma(dst, src, sem=None):
        sem = sem or s_wld
        v = P.bump(sem, 16)
        P.em("gpsimd", lambda E, d=dst, s=src, ss=sem: E.dma_start(out=d, in_=s).then_inc(ss, 16))
        return v

    def y_dma(dst, src, sem=None):
        sem = sem or s_xld
        v = P.bump(sem, 16)
        P.em("sync", lambda E, d=dst, s=src, ss=sem: E.dma_start(out=d, in_=s).then_inc(ss, 16))
        return v

    def drain_all():
        P.wait("gpsimd", s_wld, P.val(s_wld))
        P.wait("gpsimd", s_gst, P.val(s_gst))
        P.wait("gpsimd", s_cc, P.val(s_cc))
        P.wait("sync", s_xld, P.val(s_xld))
        P.wait("sync", s_st, P.val(s_st))
        P.barrier(s_bar)

    def v_op(fn, n=1):
        v = P.bump(s_ve, n)
        def f(E, fn=fn, n=n):
            last = fn(E)
            while isinstance(last, (tuple, list)):
                last = last[-1]
            last.then_inc(s_ve, n)
        P.em("vector", f)
        return v

    def v_op1(fn):
        v = P.bump(s_ve)
        P.em("vector", lambda E, fn=fn: fn(E).then_inc(s_ve, 1))
        return v

    def v_chain(fns):
        v = 0
        for fn in fns:
            v = v_op1(fn)
            P.wait("vector", s_ve, v)
        return v

    def a_op1(fn):
        v = P.bump(s_ac)
        P.em("scalar", lambda E, fn=fn: fn(E).then_inc(s_ac, 1))
        return v

    def t_op(fn, n=1):
        v = P.bump(s_mm, n)
        def f(E, fn=fn, n=n):
            last = fn(E)
            last.then_inc(s_mm, n)
        P.em("tensor", f)
        return v

    def bcast16(slice_row, dst_bank_i):
        def f(E):
            r = slice_row.unsqueeze(2).broadcast_to([1, 16, 64])
            E.matmul(bank(dst_bank_i).rearrange("p (a b) -> p a b", a=8), ones_row[0:1, :],
                     r[:, 0:8, :], start=True, stop=True)
            return E.matmul(bank(dst_bank_i + 1).rearrange("p (a b) -> p a b", a=8), ones_row[0:1, :],
                            r[:, 8:16, :], start=True, stop=True)
        return t_op(f)


    def bct(tt):
        return tt.rearrange("p (a b) -> p a b", a=16)

    g_dma(ones_row[:, :], ones_row_in[:, :])
    g_dma(ones_col_b[:, :], ones_row_in.rearrange("a b -> b a"))
    g_dma(ident_f[:, :], ident_in[:, :])
    g_dma(ident_b[:, :], ident_in[:, :])
    g_dma(tril_b[:, :], tril_in[:, :])
    g_dma(convw[:, :], convw_in[:, :])
    g_dma(convb[:, :], convb_in[:, :])
    g_dma(dtbias[:, :], dtbias_in[:, :])
    g_dma(acol[:, :], acol_in[:, :])
    g_dma(drow[:, :], drow_in[:, :])
    g_dma(bbase[:, :], bbase_in[:, :])
    g_dma(bocol[:, :], bo_in[:, :])
    g_dma(e16[:, :], e16_in[:, :])
    g_dma(gatecol[:, :], gatecol_in[:, :])
    g_dma(idxs[:, :], idx_in[:, :])
    y_dma(r_dram[:, 0:3], xhalo_in[:, :])
    y_dma(r_dram[:, 3:TH], x_in[:, :])
    drain_all()

    def emit_ln(src, dst, cols, c_start=0):
        c0 = c_start
        while c0 < cols:
            w = min(64, cols - c0)
            srcv = src[:, c0:c0 + w].rearrange("(c p) t -> p c t", p=128)
            P.wait("sync", s_ve, P.val(s_ve))
            vld = y_dma(scrA[:, 0:NKD * w].rearrange("p (c t) -> p c t", c=NKD), srcv)
            P.wait("vector", s_xld, vld)
            vv = v_op1(lambda E, w=w: E.tensor_copy(scrC[:, 0:NKD * w], scrA[:, 0:NKD * w]))
            P.wait("scalar", s_ve, vv)
            P.wait("scalar", s_st, P.val(s_st))
            va = a_op1(lambda E, w=w: E.activation(scrD[:, 0:NKD * w], scrC[:, 0:NKD * w], AF.Square))
            P.wait("tensor", s_ve, vv)
            P.wait("tensor", s_ac, va)
            def f_stats(E, w=w):
                for k in range(NKD):
                    E.matmul(bank(0)[0:1, 0:w], ones_col_b[:, 0:1], scrC[:, k * w:(k + 1) * w],
                             start=(k == 0), stop=(k == NKD - 1))
                last = None
                for k in range(NKD):
                    last = E.matmul(bank(1)[0:1, 0:w], ones_col_b[:, 0:1], scrD[:, k * w:(k + 1) * w],
                                    start=(k == 0), stop=(k == NKD - 1))
                return last
            vm = t_op(f_stats)
            P.wait("vector", s_mm, vm)
            vv2 = v_chain([
                lambda E, w=w: E.tensor_scalar_mul(rowA[0:1, 0:w], bank(0)[0:1, 0:w], 1.0 / D),
                lambda E, w=w: E.tensor_scalar_mul(rowB[0:1, 0:w], bank(1)[0:1, 0:w], 1.0 / D),
                lambda E, w=w: E.tensor_mul(rowA[0:1, 128:128 + w], rowA[0:1, 0:w], rowA[0:1, 0:w]),
                lambda E, w=w: E.tensor_sub(rowB[0:1, 0:w], rowB[0:1, 0:w], rowA[0:1, 128:128 + w]),
                lambda E, w=w: E.tensor_scalar_add(rowB[0:1, 0:w], rowB[0:1, 0:w], EPS)])
            P.wait("scalar", s_ve, vv2)
            a_op1(lambda E, w=w: E.activation(rowB[0:1, 128:128 + w], rowB[0:1, 0:w], AF.Ln))
            va2 = a_op1(lambda E, w=w: E.activation(rowB[0:1, 0:w], rowB[0:1, 128:128 + w],
                                                    AF.Exp, scale=-0.5))
            P.wait("vector", s_ac, va2)
            vv3 = v_op(lambda E, w=w: (
                E.scalar_tensor_tensor(rowA[0:1, 0:w], rowA[0:1, 0:w], -1.0, rowB[0:1, 0:w],
                                       ALU.mult, ALU.mult),))
            P.wait("tensor", s_ve, vv3)
            def f_bc(E, w=w):
                E.matmul(bank(2)[:, 0:w], ones_row[0:1, :], rowB[0:1, 0:w], start=True, stop=True)
                return E.matmul(bank(3)[:, 0:w], ones_row[0:1, :], rowA[0:1, 0:w],
                                start=True, stop=True)
            vm2 = t_op(f_bc)
            P.wait("vector", s_mm, vm2)
            v_op(lambda E, w=w: (
                E.tensor_copy(scrB[:, 0:w], bank(2)[:, 0:w]),
                E.tensor_copy(scrB[:, 512:512 + w], bank(3)[:, 0:w])))
            vv5 = v_op(lambda E, w=w: [
                (E.tensor_mul(scrA[:, k * w:(k + 1) * w], scrA[:, k * w:(k + 1) * w], scrB[:, 0:w]),
                 E.tensor_add(scrD[:, k * w:(k + 1) * w], scrA[:, k * w:(k + 1) * w], scrB[:, 512:512 + w]))
                for k in range(NKD)])
            P.wait("sync", s_ve, vv5)
            dstv = dst[:, c0:c0 + w].rearrange("(c p) t -> p c t", p=128)
            y_dma(dstv, scrD[:, 0:NKD * w].rearrange("p (c t) -> p c t", c=NKD), sem=s_st)
            c0 += w
        P.wait("sync", s_st, P.val(s_st))

    emit_ln(r_dram, xln_dram, TH)
    drain_all()
    tap("xln1", xln_dram)
    if upto == "p1":
        _finish(P, dbg, y_dma, out_d, r_dram, s_st, s_xld)
        with nc.Block() as block:
            P.emit(block)
        gctx.close()
        return nc, dbg

    mctx = ExitStack()
    M_ = mctx.enter_context
    xln_sb = M_(nc.sbuf_tensor("xln_sb", [128, NKD * TH], BF16))
    wbuf = [M_(nc.sbuf_tensor(f"wbuf{i}", [128, NKD * 128], BF16)) for i in range(2)]
    carry = M_(nc.sbuf_tensor("carry", [128, NH * HDM], F32))
    dtproc = M_(nc.sbuf_tensor("dtproc", [128, T], F32))
    cs = M_(nc.sbuf_tensor("cs", [128, T], F32))
    csT = M_(nc.sbuf_tensor("csT", [128, T], F32))
    dtT = M_(nc.sbuf_tensor("dtT", [128, T], BF16))
    dcyT = M_(nc.sbuf_tensor("dcyT", [128, T], BF16))
    expcsT = M_(nc.sbuf_tensor("expcsT", [128, T], F32))
    bcg = M_(nc.sbuf_tensor("bcg", [128, 2 * T], BF16))
    xfm = M_(nc.sbuf_tensor("xfm", [128, 8 * T], BF16))
    xbcraw = M_(nc.sbuf_tensor("xbcraw", [128, TH + 1], BF16))
    cdrow = M_(nc.sbuf_tensor("cdrow", [1, T], F32))
    cumdrow = M_(nc.sbuf_tensor("cumdrow", [1, T], F32))
    cdcol = M_(nc.sbuf_tensor("cdcol", [128, NCH], F32))
    cumdec = M_(nc.sbuf_tensor("cumdec", [128, NCH], F32))
    tr8 = M_(nc.sbuf_tensor("tr8", [8, 128], F32))

    vxln = y_dma(xln_sb[:, :].rearrange("p (c t) -> p c t", c=NKD),
                 xln_dram[:, :].rearrange("(c p) t -> p c t", p=128))
    P.wait("tensor", s_xld, vxln)

    TBS = [(0, 512), (512, 512), (1024, 3)]
    pcs = {"bank": 0, "bankv": [0] * 6, "buf": 0, "bufv": [0, 0]}

    def load_piece(col0, width):
        b = pcs["buf"]
        pcs["buf"] ^= 1
        P.wait("gpsimd", s_mm, pcs["bufv"][b])
        v = g_dma(wbuf[b][:, 0:NKD * width].rearrange("p (c m) -> p c m", c=NKD),
                  wt_inproj[:, col0:col0 + width].rearrange("(c p) m -> p c m", p=128))
        return b, v

    def mm_piece_fm(b, vld, width, evac_fn):
        P.wait("tensor", s_wld, vld)
        nfc = width // 128
        for tb0, w in TBS:
            for fc in range(nfc):
                bk = pcs["bank"]
                pcs["bank"] = (pcs["bank"] + 1) % 6
                P.wait("tensor", s_ve, pcs["bankv"][bk])
                def f_mm(E, b=b, fc=fc, tb0=tb0, w=w, bk=bk, width=width):
                    last = None
                    for k in range(NKD):
                        last = E.matmul(bank(bk)[:, 0:w],
                                        wbuf[b][:, k * width + fc * 128: k * width + fc * 128 + 128],
                                        xln_sb[:, k * TH + tb0: k * TH + tb0 + w],
                                        start=(k == 0), stop=(k == NKD - 1))
                    return last
                vm = t_op(f_mm)
                P.wait("vector", s_mm, vm)
                pcs["bankv"][bk] = evac_fn(fc, tb0, w, bk)
        pcs["bufv"][pcs["buf"] ^ 1] = P.val(s_mm)

    def mm_piece_gate(b, vld, width, hb, j):
        P.wait("tensor", s_wld, vld)
        for tc in range(8):
            bk = pcs["bank"]
            pcs["bank"] = (pcs["bank"] + 1) % 6
            P.wait("tensor", s_ve, pcs["bankv"][bk])
            def f_mm(E, b=b, tc=tc, bk=bk, width=width):
                last = None
                for k in range(NKD):
                    last = E.matmul(bank(bk)[:, 0:width],
                                    xln_sb[:, k * TH + 3 + tc * 128: k * TH + 3 + tc * 128 + 128],
                                    wbuf[b][:, k * width: (k + 1) * width],
                                    start=(k == 0), stop=(k == NKD - 1))
                return last
            vm = t_op(f_mm)
            P.wait("vector", s_mm, vm)
            slot = tc % 2
            P.wait("vector", s_st, max(0, P.val(s_st) - 16))
            vv = v_op1(lambda E, bk=bk, slot=slot, width=width:
                       E.tensor_copy(scrD[:, slot * 1024: slot * 1024 + width], bank(bk)[:, 0:width]))
            pcs["bankv"][bk] = vv
            P.wait("sync", s_ve, vv)
            y_dma(gate_dram[tc * 128:(tc + 1) * 128, hb * 1024 + j * 128: hb * 1024 + j * 128 + width],
                  scrD[:, slot * 1024: slot * 1024 + width], sem=s_st)
        pcs["bufv"][pcs["buf"] ^ 1] = P.val(s_mm)

    def conv_silu(raw_ap, conv_chunk, dst_ap):
        P.wait("vector", s_ac, P.val(s_ac))
        vv = v_op(lambda E, r=raw_ap, ch=conv_chunk: (
            E.tensor_scalar_mul(scrB[:, 0:T], r[:, 0:T], convw[:, ch * 4:ch * 4 + 1]),
            [E.scalar_tensor_tensor(scrB[:, 0:T], r[:, kk:kk + T],
                                    convw[:, ch * 4 + kk:ch * 4 + kk + 1],
                                    scrB[:, 0:T], ALU.mult, ALU.add) for kk in range(1, 4)]))
        P.wait("scalar", s_ve, vv)
        va = a_op1(lambda E, ch=conv_chunk, d=dst_ap: E.activation(
            d, scrB[:, 0:T], AF.Silu, bias=convb[:, ch:ch + 1]))
        return va

    for pi in range(16):
        col0 = 2 * INTER + pi * 128
        b, vld = load_piece(col0, 128)

        def evac_bc(fc, tb0, w, bk):
            return v_op1(lambda E, tb0=tb0, w=w, bk=bk:
                         E.tensor_copy(xbcraw[:, tb0: tb0 + w], bank(bk)[:, 0:w]))
        mm_piece_fm(b, vld, 128, evac_bc)
        gch = 64 + pi
        rowd = pi * 128
        P.wait("scalar", s_st, max(0, P.val(s_st) - 16))
        va = conv_silu(xbcraw[:, 0:TH], gch, scrD[:, (pi % 2) * T:(pi % 2) * T + T])
        P.wait("sync", s_ac, va)
        y_dma(bc_dram[rowd:rowd + 128, :], scrD[:, (pi % 2) * T:(pi % 2) * T + T], sem=s_st)

    b, vld = load_piece(2 * INTER + 2048, 128)

    def evac_dt(fc, tb0, w, bk):
        return v_op1(lambda E, tb0=tb0, w=w, bk=bk:
                     E.tensor_copy(scrA[:, tb0:tb0 + w], bank(bk)[:, 0:w]))
    mm_piece_fm(b, vld, 128, evac_dt)

    P.wait("scalar", s_ve, P.val(s_ve))
    a_op1(lambda E: E.activation(scrA[:, 1030:1030 + T], scrA[:, 3:3 + T], AF.Abs, bias=dtbias[:, 0:1]))
    a_op1(lambda E: E.activation(scrA[:, 1030:1030 + T], scrA[:, 1030:1030 + T], AF.Exp, scale=-1.0))
    a_op1(lambda E: E.activation(scrA[:, 1030:1030 + T], scrA[:, 1030:1030 + T], AF.Ln, bias=1.0))
    a4 = a_op1(lambda E: E.activation(scrA[:, 2054:2054 + T], scrA[:, 3:3 + T], AF.Relu, bias=dtbias[:, 0:1]))
    P.wait("vector", s_ac, a4)
    v_op(lambda E: (
        E.tensor_add(dtproc[:, :], scrA[:, 2054:2054 + T], scrA[:, 1030:1030 + T]),
        E.tensor_scalar(dtproc[:, :], dtproc[:, :], DT_MIN, DT_MAX, ALU.max, ALU.min),
        E.tensor_scalar_mul(scrB[:, 0:T], dtproc[:, :], acol[:, 0:1])))
    vv = v_op(lambda E: [E.tensor_tensor_scan(cs[:, c * 128:(c + 1) * 128],
                                              scrB[:, c * 128:(c + 1) * 128],
                                              scrB[:, c * 128:(c + 1) * 128],
                                              0.0, ALU.add, ALU.bypass) for c in range(NCH)])
    P.wait("scalar", s_ve, vv)
    va = 0
    for c in range(NCH):
        va = a_op1(lambda E, c=c: E.activation(cdcol[:, c:c + 1], cs[:, c * 128 + 127:c * 128 + 128], AF.Exp))
    P.wait("vector", s_ac, va)
    v_chain([lambda E: E.memset(cumdec[:, 0:1], 1.0)] +
            [(lambda E, c=c: E.tensor_mul(cumdec[:, c:c + 1], cumdec[:, c - 1:c], cdcol[:, c - 1:c]))
             for c in range(1, NCH)])
    for c in range(NCH):
        P.wait("scalar", s_mm, P.val(s_mm))
        va = a_op1(lambda E, c=c: E.activation(scrA[:, 0:128], cs[:, c * 128:(c + 1) * 128], AF.Exp,
                                               scale=-1.0, bias=cs[:, c * 128 + 127:c * 128 + 128]))
        va = a_op1(lambda E, c=c: E.activation(scrA[:, 128:256], cs[:, c * 128:(c + 1) * 128], AF.Exp))
        P.wait("tensor", s_ac, va)
        P.wait("tensor", s_ve, P.val(s_ve))
        def f_tr(E, c=c):
            E.transpose(bank(0)[:, 0:128], cs[:, c * 128:(c + 1) * 128], ident_f[:, :])
            E.transpose(bank(0)[:, 128:256], dtproc[:, c * 128:(c + 1) * 128], ident_f[:, :])
            E.transpose(bank(1)[:, 0:128], scrA[:, 0:128], ident_f[:, :])
            return E.transpose(bank(1)[:, 128:256], scrA[:, 128:256], ident_f[:, :])
        vm = t_op(f_tr)
        P.wait("vector", s_mm, vm)
        v_op(lambda E, c=c: (
            E.tensor_copy(csT[:, c * 128:(c + 1) * 128], bank(0)[:, 0:128]),
            E.tensor_copy(dtT[:, c * 128:(c + 1) * 128], bank(0)[:, 128:256]),
            E.tensor_copy(dcyT[:, c * 128:(c + 1) * 128], bank(1)[:, 0:128]),
            E.tensor_copy(expcsT[:, c * 128:(c + 1) * 128], bank(1)[:, 128:256])))
    P.wait("tensor", s_ve, P.val(s_ve))
    vm = t_op(lambda E: (E.transpose(bank(0)[0:8, 0:128], cdcol[:, 0:NCH], ident_f[:, :]),
                         E.transpose(bank(1)[0:8, 0:128], cumdec[:, 0:NCH], ident_f[:, :]))[-1])
    P.wait("vector", s_mm, vm)
    vv = v_op1(lambda E: E.tensor_copy(tr8[0:8, 0:128], bank(0)[0:8, 0:128]))
    P.wait("sync", s_ve, vv)
    vst = y_dma(cdrow[0:1, :], tr8[0:8, 0:128], sem=s_st)
    P.wait("vector", s_st, vst)
    vv = v_op1(lambda E: E.tensor_copy(tr8[0:8, 0:128], bank(1)[0:8, 0:128]))
    P.wait("sync", s_ve, vv)
    y_dma(cumdrow[0:1, :], tr8[0:8, 0:128], sem=s_st)
    v_op1(lambda E: E.memset(carry[:, :], 0.0))
    drain_all()
    tap("bc", bc_dram)
    tap("dt", dtproc)
    tap("cs", cs)
    tap("gate", gate_dram)
    tap("ydram", y_dram)

    if upto == "p2":
        _finish(P, dbg, y_dma, out_d, r_dram, s_st, s_xld)
        with nc.Block() as block:
            P.emit(block)
        mctx.close()
        gctx.close()
        return nc, dbg

    pA = pp[:, 0:1024]
    pB = pp[:, 1024:2048]
    pC = pp[:, 2048:3072]
    pS = pp[:, 3072:3584]

    for hb in range(nhb):
        for j in range(8 if ("x", hb) not in hb_skip else 0):
            col0 = INTER + hb * 1024 + j * 128
            b, vld = load_piece(col0, 128)

            def evac_x(fc, tb0, w, bk):
                return v_op1(lambda E, tb0=tb0, w=w, bk=bk:
                             E.tensor_copy(xbcraw[:, tb0: tb0 + w], bank(bk)[:, 0:w]))
            mm_piece_fm(b, vld, 128, evac_x)
            gch = hb * 8 + j
            conv_silu(xbcraw[:, 0:TH], gch, xfm[:, j * T:(j + 1) * T])
        for j in range(8 if ("gate", hb) not in hb_skip else 0):
            col0 = hb * 1024 + j * 128
            b, vld = load_piece(col0, 128)
            mm_piece_gate(b, vld, 128, hb, j)
        if ("bcg", hb) not in hb_skip:
            P.wait("sync", s_st, P.val(s_st))
            P.wait("sync", s_mm, P.val(s_mm))
            vbc = y_dma(bcg[:, 0:T], bc_dram[hb * 128:(hb + 1) * 128, :])
            vbc = y_dma(bcg[:, T:2 * T], bc_dram[1024 + hb * 128: 1024 + (hb + 1) * 128, :])
            P.wait("tensor", s_xld, vbc)
        if ("drow", hb) not in hb_skip:
            P.wait("tensor", s_ac, P.val(s_ac))
            P.wait("tensor", s_ve, P.val(s_ve))
            vm = bcast16(drow[0:1, 16 * hb:16 * hb + 16], 0)
            P.wait("vector", s_mm, vm)
            vdr = v_op(lambda E: (E.tensor_copy(scrA[:, 0:512], bank(0)),
                                  E.tensor_copy(scrA[:, 512:1024], bank(1))))

        for c in range(NCH if ("ssd", hb) not in hb_skip else 0):
            hsl = slice(c * 128 + 16 * hb, c * 128 + 16 * hb + 16)

            P.wait("tensor", s_ve, P.val(s_ve))
            vm = t_op(lambda E, c=c: E.matmul(pS[:, 0:128], bcg[:, c * 128:(c + 1) * 128],
                                              bcg[:, T + c * 128: T + c * 128 + 128],
                                              start=True, stop=True))
            P.wait("vector", s_mm, vm)
            vms = v_op1(lambda E: E.tensor_tensor(scrD[:, 0:128], pS[:, 0:128], tril_b[:, :], ALU.mult))

            def f_xtr(E, c=c):
                last = None
                for fc in range(8):
                    last = E.transpose(ppb[:, fc * 128:(fc + 1) * 128],
                                       xfm[:, fc * T + c * 128: fc * T + c * 128 + 128], ident_b[:, :])
                return last
            vm = t_op(f_xtr)
            P.wait("vector", s_mm, vm)
            vxt = v_op1(lambda E: E.tensor_copy(scrC[:, 0:1024], ppb[:, 0:1024]))
            vxd = v_op(lambda E, c=c, hb=hb: (
                E.tensor_tensor(bct(scrC[:, 1024:2048]), bct(scrC[:, 0:1024]),
                                dtT[:, c * 128 + 16 * hb: c * 128 + 16 * hb + 16]
                                .unsqueeze(2).broadcast_to([128, 16, 64]), ALU.mult),
                E.tensor_tensor(bct(scrC[:, 2048:3072]), bct(scrC[:, 1024:2048]),
                                dcyT[:, c * 128 + 16 * hb: c * 128 + 16 * hb + 16]
                                .unsqueeze(2).broadcast_to([128, 16, 64]), ALU.mult),
                E.tensor_mul(scrC[:, 3072:4096], scrC[:, 0:1024], scrA[:, 0:1024])), n=3)
            P.wait("tensor", s_ve, vxt)
            vm = t_op(lambda E, c=c: E.transpose(ppb[:, 0:128], bcg[:, c * 128:(c + 1) * 128],
                                                 ident_b[:, :]))
            P.wait("vector", s_mm, vm)
            vbt = v_op1(lambda E: E.tensor_copy(scrD[:, 128:256], ppb[:, 0:128]))
            vcb = v_op1(lambda E, hb=hb: E.tensor_copy(scrD[:, 1024:2048], carry[:, hb * 1024:(hb + 1) * 1024]))

            P.wait("tensor", s_ve, max(vbt, vcb, vxd))
            def f_big(E, c=c):
                E.matmul(pC[:, 0:512], scrD[:, 128:256], scrC[:, 2048:2560], start=True, stop=True)
                E.matmul(pC[:, 512:1024], scrD[:, 128:256], scrC[:, 2560:3072], start=True, stop=True)
                E.matmul(pB[:, 0:512], bcg[:, T + c * 128: T + c * 128 + 128], scrD[:, 1024:1536],
                         start=True, stop=True)
                E.matmul(pB[:, 512:1024], bcg[:, T + c * 128: T + c * 128 + 128], scrD[:, 1536:2048],
                         start=True, stop=True)
                if ssd_parts != "YD":
                    E.matmul(pA[:, 0:512], ident_b[:, :], scrC[:, 3072:3584], start=True, stop=True)
                    return E.matmul(pA[:, 512:1024], ident_b[:, :], scrC[:, 3584:4096], start=True, stop=True)
                return E.matmul(pB[:, 0:512], bcg[:, T + c * 128: T + c * 128 + 128], scrD[:, 1024:1536],
                                start=True, stop=True)
            vmbig = t_op(f_big)

            P.wait("vector", s_ve, vms)
            for jh in range(16):
                gh = 16 * hb + jh
                P.wait("tensor", s_ve, P.val(s_ve)) if jh == 0 else None
                vm = t_op(lambda E, c=c, gh=gh: E.matmul(
                    pS[:, 128:256],
                    ident_f[:, gh:gh + 1].broadcast_to([128, 128]),
                    cs[:, c * 128:(c + 1) * 128], start=True, stop=True))
                P.wait("vector", s_mm, vm)
                vsg = v_op1(lambda E, c=c, gh=gh: E.tensor_scalar(
                    scrA[:, 1030:1158], pS[:, 128:256], csT[:, c * 128 + gh:c * 128 + gh + 1],
                    0.0, ALU.subtract, ALU.min))
                P.wait("scalar", s_ve, vsg)
                vle = a_op1(lambda E: E.activation(scrD[:, 256:384], scrA[:, 1030:1158], AF.Exp))
                P.wait("vector", s_ac, vle)
                vmt = v_op1(lambda E: E.tensor_tensor(scrD[:, 384:512], scrD[:, 0:128],
                                                      scrD[:, 256:384], ALU.mult))
                P.wait("tensor", s_ve, vmt)
                if ssd_parts != "D":
                    t_op(lambda E, jh=jh: E.matmul(pA[:, jh * 64:(jh + 1) * 64], scrD[:, 384:512],
                                                   scrC[:, 1024 + jh * 64: 1024 + (jh + 1) * 64],
                                                   start=(ssd_parts == "YD"), stop=(jh == 15)))
            vmall = P.val(s_mm)
            P.wait("vector", s_mm, vmall)
            P.wait("vector", s_gst, P.val(s_gst))
            if ssd_parts == "A":
                ve1 = v_op(lambda E, c=c, hb=hb: (
                    E.tensor_copy(scrB[:, 0:1024], pA[:, 0:1024]),
                    E.tensor_copy(colA[:, 0:1], colA[:, 0:1])), n=2)
            elif ssd_parts == "B":
                ve1 = v_op(lambda E, c=c, hb=hb: (
                    E.tensor_tensor(bct(scrB[:, 0:1024]), bct(pB[:, 0:1024]),
                                    expcsT[:, c * 128 + 16 * hb: c * 128 + 16 * hb + 16]
                                    .unsqueeze(2).broadcast_to([128, 16, 64]), ALU.mult),
                    E.tensor_copy(colA[:, 0:1], colA[:, 0:1])), n=2)
            else:
                ve1 = v_op(lambda E, c=c, hb=hb: (
                    E.tensor_tensor(bct(scrB[:, 0:1024]), bct(pB[:, 0:1024]),
                                    expcsT[:, c * 128 + 16 * hb: c * 128 + 16 * hb + 16]
                                    .unsqueeze(2).broadcast_to([128, 16, 64]), ALU.mult),
                    E.tensor_add(scrB[:, 0:1024], scrB[:, 0:1024], pA[:, 0:1024])), n=2)
            P.wait("tensor", s_ve, ve1)
            vm = bcast16(cdrow[0:1, c * 128 + 16 * hb: c * 128 + 16 * hb + 16], 0)
            P.wait("vector", s_mm, vm)
            v_op(lambda E, hb=hb: (
                E.tensor_tensor(carry[:, hb * 1024:(hb + 1) * 1024],
                                carry[:, hb * 1024:(hb + 1) * 1024], pA[:, 0:1024], ALU.mult),
                E.tensor_tensor(carry[:, hb * 1024:(hb + 1) * 1024],
                                carry[:, hb * 1024:(hb + 1) * 1024], pC[:, 0:1024], ALU.add)), n=2)
            P.wait("gpsimd", s_ve, ve1)
            g_dma(y_dram[c * 128:(c + 1) * 128, hb * 1024:(hb + 1) * 1024], scrB[:, 0:1024], sem=s_gst)

    if "xfm" in debug_taps:
        xfm_dbg = nc.dram_tensor("dbg_xfm", [128, 8 * T], BF16, kind="ExternalOutput").ap()
        dbg["dbg_xfm"] = (None, None)
        y_dma(xfm_dbg[:, :], xfm[:, :], sem=s_st)
        dbg.pop("dbg_xfm")
    drain_all()
    if upto == "p4":
        _finish(P, dbg, y_dma, out_d, r_dram, s_st, s_xld)
        with nc.Block() as block:
            P.emit(block)
        mctx.close()
        gctx.close()
        return nc, dbg

    y_dma(cc1_in[:, :], carry[:, :], sem=s_st)
    P.wait("gpsimd", s_st, P.val(s_st))
    vcc = P.bump(s_cc)
    P.em("gpsimd", lambda E: E.collective_compute(
        "AllGather", ALU.bypass,
        replica_groups=[[0, 1], [2, 3], [4, 5], [6, 7]],
        ins=[cc1_in[:, :]], outs=[cc1_out[:, :]]).then_inc(s_cc, 1))
    P.wait("gpsimd", s_cc, vcc)

    def g_dma_dyn(dst, srcfn):
        v = P.bump(s_wld, 16)
        def f(E, d=dst, sf=srcfn):
            with E.register("rpart") as r:
                E.reg_load(r, idxs[0:1, 0:1])
                rv = RuntimeValue(val=r, min_val=0, max_val=1)
                E.dma_start(out=d, in_=sf(rv)).then_inc(s_wld, 16)
        P.em("gpsimd", f)
        return v

    vin = g_dma_dyn(carry[:, :], lambda rv: cc1_out[bass.ds(rv * 128, 128), :])
    P.wait("vector", s_wld, vin)
    v_op1(lambda E: E.tensor_scalar_mul(carry[:, :], carry[:, :], gatecol[:, 0:1]))
    drain_all()

    for hb in range(NG):
        P.wait("sync", s_mm, P.val(s_mm))
        vbc = y_dma(bcg[:, 0:T], bc_dram[hb * 128:(hb + 1) * 128, :])
        vbc = y_dma(bcg[:, T:2 * T], bc_dram[1024 + hb * 128: 1024 + (hb + 1) * 128, :])
        P.wait("tensor", s_xld, vbc)
        P.wait("sync", s_mm, P.val(s_mm))
        vnwl = y_dma(normw_row[0:1, :], normw_in[0:1, hb * 1024:(hb + 1) * 1024])
        P.wait("tensor", s_xld, vnwl)
        def f_nw(E, hb=hb):
            r = normw_row[0:1, 0:1024]
            E.matmul(bank(4), ones_row[0:1, :], r[0:1, 0:512], start=True, stop=True)
            return E.matmul(bank(5), ones_row[0:1, :], r[0:1, 512:1024], start=True, stop=True)
        P.wait("tensor", s_ve, P.val(s_ve))
        vm = t_op(f_nw)
        P.wait("vector", s_mm, vm)
        vnw = v_op(lambda E: (E.tensor_copy(evac[:, 0:512], bank(4)),
                              E.tensor_copy(evac[:, 512:1024], bank(5))))
        for c in range(NCH):
            P.wait("tensor", s_ve, P.val(s_ve))
            vm = bcast16(cumdrow[0:1, c * 128 + 16 * hb: c * 128 + 16 * hb + 16], 0)
            P.wait("vector", s_mm, vm)
            vdp = v_op1(lambda E, hb=hb: E.tensor_tensor(scrD[:, 1024:2048],
                                                         carry[:, hb * 1024:(hb + 1) * 1024],
                                                         pA[:, 0:1024], ALU.mult))
            P.wait("tensor", s_ve, vdp)
            vm = t_op(lambda E, c=c: (
                E.matmul(pB[:, 0:512], bcg[:, T + c * 128: T + c * 128 + 128], scrD[:, 1024:1536],
                         start=True, stop=True),
                E.matmul(pB[:, 512:1024], bcg[:, T + c * 128: T + c * 128 + 128], scrD[:, 1536:2048],
                         start=True, stop=True))[-1])
            P.wait("sync", s_ve, P.val(s_ve))
            vy = y_dma(scrC[:, 0:1024], y_dram[c * 128:(c + 1) * 128, hb * 1024:(hb + 1) * 1024])
            vg = y_dma(scrC[:, 1024:2048], gate_dram[c * 128:(c + 1) * 128, hb * 1024:(hb + 1) * 1024])
            P.wait("scalar", s_xld, vg)
            vsl = a_op1(lambda E: E.activation(scrD[:, 0:1024], scrC[:, 1024:2048], AF.Silu))
            P.wait("vector", s_mm, vm)
            P.wait("vector", s_xld, vy)
            P.wait("vector", s_ac, vsl)
            vyf = v_op(lambda E, c=c, hb=hb: (
                E.tensor_tensor(bct(scrB[:, 0:1024]), bct(pB[:, 0:1024]),
                                expcsT[:, c * 128 + 16 * hb: c * 128 + 16 * hb + 16]
                                .unsqueeze(2).broadcast_to([128, 16, 64]), ALU.mult),
                E.tensor_add(scrB[:, 0:1024], scrB[:, 0:1024], scrC[:, 0:1024]),
                E.tensor_mul(scrB[:, 0:1024], scrB[:, 0:1024], scrD[:, 0:1024])), n=3)
            P.wait("scalar", s_ve, vyf)
            vsq = a_op1(lambda E: E.activation(scrD[:, 2048:3072], scrB[:, 0:1024], AF.Square,
                                               accum_out=colA[:, 0:1]))
            P.wait("vector", s_ac, vsq)
            vrs = v_op1(lambda E: E.tensor_scalar(colA[:, 1:2], colA[:, 0:1], 1.0 / 1024.0, EPS,
                                                  ALU.mult, ALU.add))
            P.wait("scalar", s_ve, vrs)
            a_op1(lambda E: E.activation(colA[:, 2:3], colA[:, 1:2], AF.Ln))
            vsq2 = a_op1(lambda E: E.activation(colA[:, 3:4], colA[:, 2:3], AF.Exp, scale=-0.5))
            P.wait("vector", s_ac, vsq2)
            vyn = v_op1(lambda E: E.scalar_tensor_tensor(scrC[:, 2048:3072], scrB[:, 0:1024],
                                                         colA[:, 3:4], evac[:, 0:1024],
                                                         ALU.mult, ALU.mult))
            P.wait("tensor", s_ve, vyn)
            def f_ytr(E):
                last = None
                for fc in range(8):
                    last = E.transpose(ppb[:, fc * 128:(fc + 1) * 128],
                                       scrC[:, 2048 + fc * 128: 2048 + (fc + 1) * 128], ident_b[:, :])
                return last
            vm2 = t_op(f_ytr)
            P.wait("vector", s_mm, vm2)
            P.wait("vector", s_st, P.val(s_st))
            vev = v_op1(lambda E: E.tensor_copy(scrC[:, 3072:4096], ppb[:, 0:1024]))
            P.wait("sync", s_ve, vev)
            y_dma(yn_dram[hb * 1024:(hb + 1) * 1024, c * 128:(c + 1) * 128]
                  .rearrange("(f p) t -> p f t", p=128),
                  scrC[:, 3072:4096].rearrange("p (f t) -> p f t", f=8), sem=s_st)
    drain_all()
    tap("yn", yn_dram)
    mctx.close()

    if upto == "p6":
        _finish(P, dbg, y_dma, out_d, r_dram, s_st, s_xld)
        with nc.Block() as block:
            P.emit(block)
        gctx.close()
        return nc, dbg

    octx = ExitStack()
    O_ = octx.enter_context
    NKY = INTER // 128
    yrhs = O_(nc.sbuf_tensor("yrhs", [128, NKY * 512], BF16))
    owbuf = [O_(nc.sbuf_tensor(f"owbuf{i}", [128, NKY * 256], BF16)) for i in range(2)]
    ost = {"buf": 0, "bufv": [0, 0], "bank": 0, "bankv": [0] * 8}

    def oload(col0):
        b = ost["buf"]
        ost["buf"] ^= 1
        P.wait("gpsimd", s_mm, ost["bufv"][b])
        v = g_dma(owbuf[b][:, :].rearrange("p (c m) -> p c m", c=NKY),
                  wt_outproj[:, col0:col0 + 256].rearrange("(c p) m -> p c m", p=128))
        return b, v

    for tb in range(2):
        P.wait("sync", s_mm, P.val(s_mm))
        vy = y_dma(yrhs[:, :].rearrange("p (c t) -> p c t", c=NKY),
                   yn_dram[:, tb * 512:(tb + 1) * 512].rearrange("(c p) t -> p c t", p=128))
        P.wait("tensor", s_xld, vy)
        for pi in range(16):
            b, vld = oload(pi * 256)
            P.wait("tensor", s_wld, vld)
            for mb in range(2):
                bk = ost["bank"]
                ost["bank"] = (ost["bank"] + 1) % 8
                P.wait("tensor", s_ve, ost["bankv"][bk])
                def f_mm(E, b=b, mb=mb, bk=bk):
                    last = None
                    for k in range(NKY):
                        last = E.matmul(bank(bk),
                                        owbuf[b][:, k * 256 + mb * 128: k * 256 + mb * 128 + 128],
                                        yrhs[:, k * 512:(k + 1) * 512],
                                        start=(k == 0), stop=(k == NKY - 1))
                    return last
                vm = t_op(f_mm)
                row = pi * 256 + mb * 128
                P.wait("sync", s_ve, P.val(s_ve))
                vr = y_dma(scrA[:, 0:512], r_dram[row:row + 128, 3 + tb * 512: 3 + (tb + 1) * 512])
                P.wait("vector", s_mm, vm)
                P.wait("vector", s_xld, vr)
                P.wait("vector", s_st, max(0, P.val(s_st) - 2 * 16))
                slot = (pi * 2 + mb) % 2
                vv = v_op1(lambda E, bk=bk, slot=slot: E.tensor_tensor(
                    evac[:, slot * 512:(slot + 1) * 512], bank(bk), scrA[:, 0:512], ALU.add))
                ost["bankv"][bk] = vv
                P.wait("sync", s_ve, vv)
                y_dma(r_dram[row:row + 128, 3 + tb * 512: 3 + (tb + 1) * 512],
                      evac[:, slot * 512:(slot + 1) * 512], sem=s_st)
            ost["bufv"][b] = P.val(s_mm)
    drain_all()
    octx.close()
    tap_now("r1", r_dram)
    if upto == "p7":
        _finish(P, dbg, y_dma, out_d, r_dram, s_st, s_xld)
        with nc.Block() as block:
            P.emit(block)
        gctx.close()
        return nc, dbg

    fctx = ExitStack()
    F_ = fctx.enter_context
    xln_f = F_(nc.sbuf_tensor("xln_f", [128, NKD * T], BF16))
    fwbuf = [F_(nc.sbuf_tensor(f"fwbuf{i}", [128, NKD * 256], BF16)) for i in range(2)]
    laA = F_(nc.sbuf_tensor("laA", [128, NKD * 128], BF16))
    laB = F_(nc.sbuf_tensor("laB", [128, D], BF16))
    wrt = F_(nc.sbuf_tensor("wrt", [128, NKD * 8], BF16))
    hw_sb = F_(nc.sbuf_tensor("hw_sb", [128, T], BF16))
    wtm = F_(nc.sbuf_tensor("wtm", [128, 64], F32))
    wfm = F_(nc.sbuf_tensor("wfm", [8, T], F32))
    amask_sb = F_(nc.sbuf_tensor("amask_sb", [128, 4 * 512], F32))
    qh = F_(nc.sbuf_tensor("qh", [128, 2 * T], BF16))
    kh = F_(nc.sbuf_tensor("kh", [128, 2 * T], BF16))
    vh = F_(nc.sbuf_tensor("vh", [128, 2 * T], BF16))
    ctxh = F_(nc.sbuf_tensor("ctxh", [128, 2 * T], BF16))
    pt_sb = F_(nc.sbuf_tensor("pt_sb", [128, 2 * 512], BF16))
    rrow = F_(nc.sbuf_tensor("rrow", [1, 512], F32))

    fst = {"buf": 0, "bufv": [0, 0], "bank": 0, "bankv": [0] * 8}

    def fload(wsrc, col0, width=256):
        b = fst["buf"]
        fst["buf"] ^= 1
        P.wait("gpsimd", s_mm, fst["bufv"][b])
        v = g_dma(fwbuf[b][:, 0:NKD * width].rearrange("p (c m) -> p c m", c=NKD),
                  wsrc[:, col0:col0 + width].rearrange("(c p) m -> p c m", p=128))
        return b, v

    def fsweep(wsrc, npieces, evac_fn, extra_mm=None):
        for pi in range(npieces):
            b, vld = fload(wsrc, pi * 256)
            P.wait("tensor", s_wld, vld)
            for mb in range(2):
                for tb in range(2):
                    bk = fst["bank"]
                    fst["bank"] = (fst["bank"] + 1) % 8
                    P.wait("tensor", s_ve, fst["bankv"][bk])
                    def f_mm2(E, b=b, mb=mb, tb=tb, bk=bk, pi=pi):
                        last = None
                        for k in range(NKD):
                            last = E.matmul(bank(bk),
                                            fwbuf[b][:, k * 256 + mb * 128: k * 256 + mb * 128 + 128],
                                            xln_f[:, k * T + tb * 512: k * T + tb * 512 + 512],
                                            start=(k == 0), stop=(extra_mm is None and k == NKD - 1))
                        if extra_mm is not None:
                            last = extra_mm(E, pi, mb, tb, bk)
                        return last
                    vm = t_op(f_mm2)
                    fst["bankv"][bk] = evac_fn(pi, mb, tb, bk, vm)
            fst["bufv"][b] = P.val(s_mm)

    def load_xln_f():
        P.wait("sync", s_mm, P.val(s_mm))
        P.wait("sync", s_ac, P.val(s_ac))
        v = y_dma(xln_f[:, :].rearrange("p (c t) -> p c t", c=NKD),
                  xln_dram[:, 3:TH].rearrange("(c p) t -> p c t", p=128))
        P.wait("tensor", s_xld, v)
        P.wait("vector", s_xld, v)
        return v

    def emit_ffn():
        g_dma(wrt[:, :].rearrange("p (c m) -> p c m", c=NKD),
              wrout_in[:, :].rearrange("(c p) m -> p c m", p=128))
        g_dma(laA[:, :].rearrange("p (c m) -> p c m", c=NKD),
              loraA_in[:, :].rearrange("(c p) m -> p c m", p=128))
        vw = g_dma(laB[:, :], loraB_in[:, :])
        P.wait("tensor", s_wld, vw)
        for tc in range(8):
            P.wait("tensor", s_ve, P.val(s_ve))
            P.wait("tensor", s_gp, P.val(s_gp))
            def f_rt(E, tc=tc):
                last = None
                for k in range(NKD):
                    last = E.matmul(bank(6)[:, 0:8],
                                    xln_f[:, k * T + tc * 128: k * T + tc * 128 + 128],
                                    wrt[:, k * 8:(k + 1) * 8],
                                    start=(k == 0), stop=(k == NKD - 1))
                return last
            vm = t_op(f_rt)
            P.wait("vector", s_mm, vm)
            vl = v_chain([
                lambda E: E.tensor_copy(scrB[:, 0:8], bank(6)[:, 0:8]),
                lambda E: E.reduce_max(colA[:, 0:1], bank(6)[:, 0:8], axis=AX, negate=True)])
            P.wait("scalar", s_ve, vl)
            ve_ = a_op1(lambda E: E.activation(scrB[:, 8:16], scrB[:, 0:8], AF.Exp,
                                               bias=colA[:, 0:1]))
            P.wait("vector", s_ac, ve_)
            vm1 = v_chain([
                lambda E: E.tensor_single_scalar(scrB[:, 16:24], scrB[:, 8:16], 1.0, ALU.is_ge),
                lambda E: E.tensor_sub(scrB[:, 24:32], scrB[:, 8:16], scrB[:, 16:24]),
                lambda E: E.reduce_max(colA[:, 1:2], scrB[:, 24:32], axis=AX)])
            P.wait("scalar", s_ve, vm1)
            vb = a_op1(lambda E: E.copy(colA[:, 2:3], colA[:, 1:2]))
            P.wait("vector", s_ac, vb)
            vsel = v_chain([
                lambda E: E.tensor_single_scalar(scrB[:, 32:40], scrB[:, 24:32], colA[:, 2:3], ALU.is_ge),
                lambda E: E.tensor_add(scrB[:, 32:40], scrB[:, 32:40], scrB[:, 16:24]),
                lambda E: E.tensor_mul(scrB[:, 40:48], scrB[:, 8:16], scrB[:, 32:40]),
                lambda E: E.reduce_sum(colA[:, 1:2], scrB[:, 40:48], axis=AX),
                lambda E: E.reciprocal(colA[:, 2:3], colA[:, 1:2])])
            P.wait("scalar", s_ve, vsel)
            vb2 = a_op1(lambda E: E.copy(colA[:, 3:4], colA[:, 2:3]))
            P.wait("vector", s_ac, vb2)
            vdv = v_op1(lambda E, tc=tc: E.tensor_single_scalar(
                wtm[:, tc * 8:(tc + 1) * 8], scrB[:, 40:48], colA[:, 3:4], ALU.mult))
            P.wait("tensor", s_ve, vdv)
            vm2 = t_op(lambda E, tc=tc: E.transpose(bank(7)[0:8, 0:128], wtm[:, tc * 8:(tc + 1) * 8],
                                                    ident_f[:, :]))
            P.wait("vector", s_mm, vm2)
            v_op1(lambda E, tc=tc: E.tensor_copy(wfm[0:8, tc * 128:(tc + 1) * 128],
                                                 bank(7)[0:8, 0:128]))
        if "wtm" in debug_taps and "dbg_wtm" not in dbg:
            o = nc.dram_tensor("dbg_wtm", [128, 64], F32, kind="ExternalOutput").ap()
            P.wait("sync", s_ve, P.val(s_ve))
            y_dma(o[:, :], wtm[:, :], sem=s_st)
            dbg["dbg_wtm"] = (o, None)
        P.wait("tensor", s_ve, P.val(s_ve))
        vm = t_op(lambda E: (E.matmul(bank(4), e16[:, :], wfm[0:8, 0:512], start=True, stop=True),
                             E.matmul(bank(5), e16[:, :], wfm[0:8, 512:1024],
                                      start=True, stop=True))[-1])
        P.wait("vector", s_mm, vm)
        v_op(lambda E: (E.tensor_copy(scrA[:, 0:512], bank(4)),
                        E.tensor_copy(scrA[:, 512:1024], bank(5))), n=1)
        for tb in range(2):
            P.wait("tensor", s_ve, P.val(s_ve))
            def f_h(E, tb=tb):
                last = None
                for k in range(NKD):
                    last = E.matmul(bank(tb), laA[:, k * 128:(k + 1) * 128],
                                    xln_f[:, k * T + tb * 512: k * T + tb * 512 + 512],
                                    start=(k == 0), stop=(k == NKD - 1))
                return last
            vm = t_op(f_h)
            P.wait("vector", s_mm, vm)
            v_op1(lambda E, tb=tb: E.tensor_tensor(hw_sb[:, tb * 512:(tb + 1) * 512],
                                                   bank(tb), scrA[:, tb * 512:(tb + 1) * 512],
                                                   ALU.mult))
        def extra_lora(E, pi, mb, tb, bk):
            return E.matmul(bank(bk), laB[:, pi * 256 + mb * 128: pi * 256 + mb * 128 + 128],
                            hw_sb[:, tb * 512:(tb + 1) * 512], start=False, stop=True)

        def evac_ffn(pi, mb, tb, bk, vm):
            row = pi * 256 + mb * 128
            ch = pi * 2 + mb
            P.wait("sync", s_ve, P.val(s_ve))
            vr = y_dma(scrA[:, 1536 + (tb % 2) * 512: 1536 + (tb % 2) * 512 + 512],
                       r_dram[row:row + 128, 3 + tb * 512: 3 + (tb + 1) * 512])
            P.wait("vector", s_mm, vm)
            P.wait("vector", s_xld, vr)
            P.wait("vector", s_st, max(0, P.val(s_st) - 2 * 16))
            slot = (pi * 2 + mb + tb) % 2
            vv = v_op1(lambda E, bk=bk, slot=slot, ch=ch, tb=tb: E.scalar_tensor_tensor(
                evac[:, slot * 512:(slot + 1) * 512], bank(bk), bbase[:, ch:ch + 1],
                scrA[:, 1536 + (tb % 2) * 512: 1536 + (tb % 2) * 512 + 512],
                ALU.add, ALU.add))
            P.wait("sync", s_ve, vv)
            y_dma(r_dram[row:row + 128, 3 + tb * 512: 3 + (tb + 1) * 512],
                  evac[:, slot * 512:(slot + 1) * 512], sem=s_st)
            return vv
        fsweep(wbase_in, 16, evac_ffn, extra_mm=extra_lora)

    emit_ln(r_dram, xln_dram, TH, c_start=3)
    drain_all()
    load_xln_f()
    emit_ffn()
    drain_all()
    tap_now("r2", r_dram)
    if upto == "p8":
        _finish(P, dbg, y_dma, out_d, r_dram, s_st, s_xld)
        with nc.Block() as block:
            P.emit(block)
        fctx.close()
        gctx.close()
        return nc, dbg

    emit_ln(r_dram, xln_dram, TH, c_start=3)
    drain_all()
    load_xln_f()
    g_dma(amask_sb[:, :], amask_in[:, :])

    def evac_q(pi, mb, tb, bk, vm):
        row = pi * 256 + mb * 128
        P.wait("vector", s_mm, vm)
        P.wait("vector", s_st, max(0, P.val(s_st) - 2 * 16))
        slot = (pi * 2 + mb + tb) % 2
        vv = v_op1(lambda E, bk=bk, slot=slot: E.tensor_copy(
            scrD[:, slot * 512:(slot + 1) * 512], bank(bk)))
        P.wait("sync", s_ve, vv)
        y_dma(q_cc[row:row + 128, tb * 512:(tb + 1) * 512],
              scrD[:, slot * 512:(slot + 1) * 512], sem=s_st)
        return vv
    fsweep(wq_in, 16, evac_q)

    for g in range(8):
        b, vld = fload(wkv_in, g * 256)
        P.wait("tensor", s_wld, vld)
        for tb in range(2):
            bk = fst["bank"]
            fst["bank"] = (fst["bank"] + 1) % 8
            P.wait("tensor", s_ve, fst["bankv"][bk])
            def f_k(E, b=b, tb=tb, bk=bk):
                last = None
                for k in range(NKD):
                    last = E.matmul(bank(bk), fwbuf[b][:, k * 256: k * 256 + 128],
                                    xln_f[:, k * T + tb * 512: k * T + tb * 512 + 512],
                                    start=(k == 0), stop=(k == NKD - 1))
                return last
            vm = t_op(f_k)
            P.wait("vector", s_mm, vm)
            P.wait("vector", s_st, max(0, P.val(s_st) - 2 * 16))
            slot = tb % 2
            vv = v_op1(lambda E, bk=bk, slot=slot: E.tensor_copy(
                scrD[:, slot * 512:(slot + 1) * 512], bank(bk)))
            fst["bankv"][bk] = vv
            P.wait("sync", s_ve, vv)
            y_dma(k_cc[g * 128:(g + 1) * 128, tb * 512:(tb + 1) * 512],
                  scrD[:, slot * 512:(slot + 1) * 512], sem=s_st)
        for tc in range(8):
            bk = fst["bank"]
            fst["bank"] = (fst["bank"] + 1) % 8
            P.wait("tensor", s_ve, fst["bankv"][bk])
            def f_v(E, b=b, tc=tc, bk=bk):
                last = None
                for k in range(NKD):
                    last = E.matmul(bank(bk)[:, 0:128],
                                    xln_f[:, k * T + tc * 128: k * T + tc * 128 + 128],
                                    fwbuf[b][:, k * 256 + 128: k * 256 + 256],
                                    start=(k == 0), stop=(k == NKD - 1))
                return last
            vm = t_op(f_v)
            P.wait("vector", s_mm, vm)
            P.wait("vector", s_st, max(0, P.val(s_st) - 2 * 16))
            slot = tc % 2
            vv = v_op1(lambda E, bk=bk, slot=slot: E.tensor_copy(
                scrD[:, slot * 512: slot * 512 + 128], bank(bk)[:, 0:128]))
            fst["bankv"][bk] = vv
            P.wait("sync", s_ve, vv)
            y_dma(v_cc[tc * 128:(tc + 1) * 128, g * 128:(g + 1) * 128],
                  scrD[:, slot * 512: slot * 512 + 128], sem=s_st)
        fst["bufv"][b] = P.val(s_mm)
    drain_all()

    vcc = P.bump(s_cc)
    P.em("gpsimd", lambda E: E.collective_compute(
        "AllGather", ALU.bypass,
        replica_groups=[[0, 1], [2, 3], [4, 5], [6, 7]],
        ins=[q_cc[:, :]], outs=[q_g[:, :]]).then_inc(s_cc, 1))
    for src_t, dst_t in ((k_cc, k_g), (v_cc, v_g)):
        vcc = P.bump(s_cc)
        P.em("gpsimd", lambda E, s=src_t, d=dst_t: E.collective_compute(
            "AllGather", ALU.bypass,
            replica_groups=[[0, 1, 2, 3, 4, 5, 6, 7]],
            ins=[s[:, :]], outs=[d[:, :]]).then_inc(s_cc, 1))
    P.wait("gpsimd", s_cc, P.val(s_cc))

    _regctr = [0]

    def dyn_dma(dst, srcfn, idx_col, scale, extra=0, maxv=2048):
        v = P.bump(s_wld, 16)
        _regctr[0] += 1
        _rn = f"rdy{_regctr[0]}"
        def f(E, d=dst, sf=srcfn, _rn=_rn, maxv=maxv):
            with E.register(_rn) as r:
                E.reg_load(r, idxs[0:1, idx_col:idx_col + 1])
                rv = RuntimeValue(val=r, min_val=0, max_val=maxv)
                E.dma_start(out=d, in_=sf(rv * scale + extra)).then_inc(s_wld, 16)
        P.em("gpsimd", f)
        return v

    for r in range(2):
        dyn_dma(q_gg[r * 2048:(r + 1) * 2048, :],
                lambda off, r=r: q_g[bass.ds(off, 2048), :], 3, 1, r * D, maxv=2048)
    for slot in range(8):
        dyn_dma(k_gg[slot * 128:(slot + 1) * 128, :],
                lambda off, slot=slot: k_g[bass.ds(off, 128), :], 5, 1, slot * AG * 128,
                maxv=896)
    vq = dyn_dma(v_gg[:, :], lambda off: v_g[:, bass.ds(off, 128)], 5, 1, 0, maxv=896)
    P.wait("sync", s_wld, P.val(s_wld))

    for i in range(16):
        P.wait("sync", s_mm, P.val(s_mm))
        bb = i // 4
        vq = 0
        for r in range(2):
            slot = 2 * bb + r
            vq = y_dma(qh[:, r * T:(r + 1) * T], q_gg[r * 2048 + i * 128: r * 2048 + (i + 1) * 128, :])
            vq = y_dma(kh[:, r * T:(r + 1) * T], k_gg[slot * 128:(slot + 1) * 128, :])
            vq = y_dma(vh[:, r * T:(r + 1) * T].rearrange("p (tc q) -> p tc q", tc=8),
                       v_gg[slot * T:(slot + 1) * T, :]
                       .rearrange("(tc p) hd -> p tc hd", p=128))
        P.wait("tensor", s_xld, vq)
        P.wait("scalar", s_xld, vq)
        P.wait("scalar", s_ve, P.val(s_ve))
        va = a_op1(lambda E: E.activation(scrD[:, 0:2048], qh[:, :], AF.Square))
        va = a_op1(lambda E: E.activation(scrD[:, 2048:3072], kh[:, 0:1024], AF.Square))
        va = a_op1(lambda E: E.activation(scrC[:, 0:1024], kh[:, 1024:2048], AF.Square))
        P.wait("tensor", s_ac, va)
        def f_nrm2(E):
            last = None
            for j in range(4):
                last = E.matmul(pp[0:1, j * 512:(j + 1) * 512], ones_col_b[:, 0:1],
                                scrD[:, j * 512:(j + 1) * 512], start=True, stop=True)
            for j in range(2):
                last = E.matmul(pp[0:1, 2048 + j * 512: 2048 + (j + 1) * 512],
                                ones_col_b[:, 0:1], scrD[:, 2048 + j * 512: 2048 + (j + 1) * 512],
                                start=True, stop=True)
            for j in range(2):
                last = E.matmul(pp[0:1, 3072 + j * 512: 3072 + (j + 1) * 512],
                                ones_col_b[:, 0:1], scrC[:, j * 512:(j + 1) * 512],
                                start=True, stop=True)
            return last
        vm = t_op(f_nrm2)
        P.wait("vector", s_mm, vm)
        vv = v_chain([
            lambda E: E.reduce_max(rowA[0:1, 0:1], pp[0:1, 0:2048], axis=AX),
            lambda E: E.reduce_max(rowA[0:1, 1:2], pp[0:1, 2048:4096], axis=AX)])
        P.wait("scalar", s_ve, vv)
        va = a_op1(lambda E: E.activation(rowA[0:1, 2:3], rowA[0:1, 0:1], AF.Ln))
        va = a_op1(lambda E: E.activation(rowA[0:1, 3:4], rowA[0:1, 1:2], AF.Ln))
        P.wait("vector", s_ac, va)
        vv = v_chain([
            lambda E: E.tensor_add(rowA[0:1, 4:5], rowA[0:1, 2:3], rowA[0:1, 3:4]),
            lambda E: E.tensor_scalar_add(rowA[0:1, 4:5], rowA[0:1, 4:5], 2.0 * math.log(ASCALE))])
        P.wait("scalar", s_ve, vv)
        va = a_op1(lambda E: E.activation(rowA[0:1, 5:6], rowA[0:1, 4:5], AF.Exp, scale=0.5))
        P.wait("vector", s_ac, va)
        vv = v_op1(lambda E: E.tensor_scalar_mul(rowA[0:1, 6:7], rowA[0:1, 5:6], -1.0))
        P.wait("tensor", s_ve, vv)
        vm = t_op(lambda E: E.matmul(bank(6)[:, 0:1], ones_row[0:1, :], rowA[0:1, 6:7],
                                     start=True, stop=True))
        P.wait("vector", s_mm, vm)
        vneg = v_op1(lambda E: E.tensor_copy(colB[:, 0:1] if False else colA[:, 0:1], bank(6)[:, 0:1]))

        for bnd in range(4):
            nki = 4 * (bnd + 1)
            for ki in range(nki):
                sbk = ki % 2
                P.wait("tensor", s_ve, P.val(s_ve)) if ki < 2 else None
                def f_st(E, ki=ki, bnd=bnd, sbk=sbk):
                    return E.matmul(bank(sbk), kh[:, ki * 128:(ki + 1) * 128],
                                    qh[:, bnd * 512:(bnd + 1) * 512], start=True, stop=True)
                vm = t_op(f_st)
                rel_ = ki - 4 * bnd
                if rel_ >= 0:
                    P.wait("vector", s_mm, vm)
                    vv = v_op1(lambda E, sbk=sbk, rel_=rel_: E.tensor_tensor(
                        scrA[:, 0:512], bank(sbk), amask_sb[:, rel_ * 512:(rel_ + 1) * 512],
                        ALU.add))
                    P.wait("scalar", s_ve, vv)
                    va = a_op1(lambda E, ki=ki: E.activation(
                        pt_sb[:, (ki % 2) * 512:(ki % 2) * 512 + 512], scrA[:, 0:512], AF.Exp,
                        scale=ASCALE, bias=colA[:, 0:1]))
                else:
                    P.wait("scalar", s_mm, vm)
                    va = a_op1(lambda E, ki=ki, sbk=sbk: E.activation(
                        pt_sb[:, (ki % 2) * 512:(ki % 2) * 512 + 512], bank(sbk), AF.Exp,
                        scale=ASCALE, bias=colA[:, 0:1]))
                P.wait("tensor", s_ac, va)
                def f_acc(E, ki=ki, nki=nki):
                    E.matmul(bank(6)[0:1, 0:512], ones_col_b[:, 0:1],
                             pt_sb[:, (ki % 2) * 512:(ki % 2) * 512 + 512],
                             start=(ki == 0), stop=(ki == nki - 1))
                    return E.matmul(pp[:, 2048:2560], vh[:, ki * 128:(ki + 1) * 128],
                                    pt_sb[:, (ki % 2) * 512:(ki % 2) * 512 + 512],
                                    start=(ki == 0), stop=(ki == nki - 1))
                vm = t_op(f_acc)
            P.wait("vector", s_mm, vm)
            vv = v_op1(lambda E: E.reciprocal(rrow[0:1, 0:512], bank(6)[0:1, 0:512]))
            P.wait("tensor", s_ve, vv)
            vm2 = t_op(lambda E: E.matmul(bank(7), ones_row[0:1, :], rrow[0:1, 0:512],
                                          start=True, stop=True))
            P.wait("vector", s_mm, vm2)
            vv2 = v_op(lambda E, bnd=bnd: (
                E.tensor_copy(scrA[:, 512:1024], bank(7)),
                E.tensor_tensor(ctxh[:, bnd * 512:(bnd + 1) * 512], pp[:, 2048:2560],
                                scrA[:, 512:1024], ALU.mult)), n=2)
        P.wait("sync", s_ve, P.val(s_ve))
        y_dma(ctx_dram[i * 128:(i + 1) * 128, :], ctxh[:, :], sem=s_st)
    P.wait("gpsimd", s_st, P.val(s_st))
    vdy = P.bump(s_gst, 16)
    def f_c3(E):
        with E.register("rc3") as r:
            E.reg_load(r, idxs[0:1, 6:7])
            rv = RuntimeValue(val=r, min_val=0, max_val=1024)
            E.dma_start(out=cc3_in[:, :],
                        in_=ctx_dram[:, bass.ds(rv, 1024)]).then_inc(s_gst, 16)
    P.em("gpsimd", f_c3)
    drain_all()

    vcc = P.bump(s_cc)
    P.em("gpsimd", lambda E: E.collective_compute(
        "AllGather", ALU.bypass,
        replica_groups=[[0, 1], [2, 3], [4, 5], [6, 7]],
        ins=[cc3_in[:, :]], outs=[cc3_out[:, :]]).then_inc(s_cc, 1))
    P.wait("gpsimd", s_cc, vcc)
    v1 = P.bump(s_wld, 16)
    def f_as1(E):
        with E.register("ra1") as r:
            E.reg_load(r, idxs[0:1, 3:4])
            rv = RuntimeValue(val=r, min_val=0, max_val=2048)
            with E.register("ra2") as r2:
                E.reg_load(r2, idxs[0:1, 7:8])
                rv2 = RuntimeValue(val=r2, min_val=0, max_val=1024)
                E.dma_start(out=ctx_all[bass.ds(rv, 2048), :],
                            in_=ctx_dram[:, bass.ds(rv2, 1024)]).then_inc(s_wld, 16)
    P.em("gpsimd", f_as1)
    v1 = P.bump(s_wld, 16)
    def f_as2(E):
        with E.register("ra3") as r:
            E.reg_load(r, idxs[0:1, 4:5])
            rv = RuntimeValue(val=r, min_val=0, max_val=2048)
            with E.register("ra4") as r2:
                E.reg_load(r2, idxs[0:1, 0:1])
                rv2 = RuntimeValue(val=r2, min_val=0, max_val=1)
                E.dma_start(out=ctx_all[bass.ds(rv, 2048), :],
                            in_=cc3_out[bass.ds(rv2 * 2048, 2048), :]).then_inc(s_wld, 16)
    P.em("gpsimd", f_as2)
    drain_all()

    P.wait("sync", s_mm, P.val(s_mm))
    vctx = y_dma(xln_f[:, :].rearrange("p (c t) -> p c t", c=NKD),
                 ctx_all[:, :].rearrange("(c p) t -> p c t", p=128))
    P.wait("tensor", s_xld, vctx)

    def evac_wo(pi, mb, tb, bk, vm):
        row = pi * 256 + mb * 128
        ch = pi * 2 + mb
        P.wait("sync", s_ve, P.val(s_ve))
        vr = y_dma(scrA[:, 1536 + (tb % 2) * 512: 1536 + (tb % 2) * 512 + 512],
                   r_dram[row:row + 128, 3 + tb * 512: 3 + (tb + 1) * 512])
        P.wait("vector", s_mm, vm)
        P.wait("vector", s_xld, vr)
        P.wait("vector", s_st, max(0, P.val(s_st) - 2 * 16))
        slot = (pi * 2 + mb + tb) % 2
        vv = v_op1(lambda E, bk=bk, slot=slot, ch=ch, tb=tb: E.scalar_tensor_tensor(
            evac[:, slot * 512:(slot + 1) * 512], bank(bk), bocol[:, ch:ch + 1],
            scrA[:, 1536 + (tb % 2) * 512: 1536 + (tb % 2) * 512 + 512],
            ALU.add, ALU.add))
        P.wait("sync", s_ve, vv)
        y_dma(r_dram[row:row + 128, 3 + tb * 512: 3 + (tb + 1) * 512],
              evac[:, slot * 512:(slot + 1) * 512], sem=s_st)
        return vv
    fsweep(wo_in, 16, evac_wo)
    drain_all()
    tap_now("r3", r_dram)
    if upto == "p14":
        _finish(P, dbg, y_dma, out_d, r_dram, s_st, s_xld)
        with nc.Block() as block:
            P.emit(block)
        fctx.close()
        gctx.close()
        return nc, dbg

    emit_ln(r_dram, xln_dram, TH, c_start=3)
    drain_all()
    load_xln_f()
    emit_ffn()
    drain_all()
    fctx.close()

    _finish(P, dbg, y_dma, out_d, r_dram, s_st, s_xld)
    with nc.Block() as block:
        P.emit(block)
    gctx.close()
    return nc, dbg


def _finish(P, dbg, y_dma, out_d, r_dram, s_st, s_xld):
    for name, (o, apx) in dbg.items():
        if apx is not None:
            y_dma(o[:, :], apx[:, :], sem=s_st)
    y_dma(out_d[:, :], r_dram[:, 3:TH], sem=s_st)
    P.wait("sync", s_st, P.val(s_st))
    P.wait("sync", s_xld, P.val(s_xld))



def _mk_inputs(inputs):
    x = np.asarray(inputs["L0_attn_a_hidden_states"], dtype=np.float32)
    wip = np.ascontiguousarray(np.asarray(inputs["L0_attn_a_in_proj_w"], np.float32).T)
    convw = np.asarray(inputs["L0_attn_a_conv_w"], np.float32).reshape(CONV_DIM, 4)
    convw = np.ascontiguousarray(convw.reshape(80, 128, 4).transpose(1, 0, 2).reshape(128, 320))
    convb = np.ascontiguousarray(np.asarray(inputs["L0_attn_a_conv_b"], np.float32)
                                 .reshape(80, 128).T)
    dtb = np.asarray(inputs["L0_attn_a_dt_bias"], np.float32).reshape(128, 1)
    Dv = np.asarray(inputs["L0_attn_a_D"], np.float32).reshape(1, 128)
    normw = np.asarray(inputs["L0_attn_a_norm_w"], np.float32).reshape(1, INTER)
    wop = np.ascontiguousarray(np.asarray(inputs["L0_attn_a_out_proj_w"], np.float32).T)
    wq = np.ascontiguousarray(np.asarray(inputs["L0_attn_b_Wq"], np.float32))
    wkv = np.ascontiguousarray(np.asarray(inputs["L0_attn_b_Wkv"], np.float32))
    wo = np.ascontiguousarray(np.asarray(inputs["L0_attn_b_Wo"], np.float32))
    bo = np.ascontiguousarray(np.asarray(inputs["L0_attn_b_bo"], np.float32).reshape(NKD, 128).T)
    wbase = np.ascontiguousarray(np.asarray(inputs["L0_ffn_W_base"], np.float32).T)
    bbase = np.ascontiguousarray(np.asarray(inputs["L0_ffn_b_base"], np.float32).reshape(NKD, 128).T)
    wrout = np.ascontiguousarray(np.asarray(inputs["L0_ffn_W_router"], np.float32).T)
    loraA = np.ascontiguousarray(np.asarray(inputs["L0_ffn_lora_A"], np.float32)
                                 .transpose(2, 0, 1).reshape(D, 128))
    loraB = np.ascontiguousarray(np.asarray(inputs["L0_ffn_lora_B"], np.float32)
                                 .transpose(0, 2, 1).reshape(128, D))

    acol = -(np.arange(1, 129, dtype=np.float32)).reshape(128, 1)
    e16v = np.zeros((E_EXP, 128), np.float32)
    for e in range(E_EXP):
        e16v[e, e * R_LORA:(e + 1) * R_LORA] = LORA_SCALE
    ones_row = np.ones((1, 128), np.float32)
    ident = np.eye(128, dtype=np.float32)
    sidx = np.arange(128).reshape(128, 1)
    lidx = np.arange(128).reshape(1, 128)
    tril = (lidx >= sidx).astype(np.float32)
    am = np.zeros((128, 4, 512), np.float32)
    for r in range(4):
        kk = np.arange(128).reshape(128, 1)
        qq = np.arange(512).reshape(1, 512)
        am[:, r, :] = np.where(r * 128 + kk > qq, -1e9, 0.0)
    am = am.reshape(128, 4 * 512)

    shared = {
        "wt_inproj": wip, "convw": convw, "convb": convb, "dtbias_col": dtb,
        "a_col": acol, "d_row": Dv, "normw_row": normw, "wt_outproj": wop,
        "wq": wq, "wkv": wkv, "wo": wo, "bo_col": bo, "wt_base": wbase,
        "bbase_col": bbase, "wt_router": wrout, "loraA_t": loraA, "loraB_t": loraB,
        "e16": e16v, "ones_row": ones_row, "ident_f": ident, "tril_sl": tril,
        "attn_masks": np.ascontiguousarray(am),
    }
    ims = []
    for c in range(8):
        s, r = c // 2, c % 2
        xs = np.ascontiguousarray(x[s, r * T:(r + 1) * T, :].T)
        halo = (np.ascontiguousarray(x[s, r * T - 3:r * T, :].T) if r == 1
                else np.zeros((D, 3), np.float32))
        hbase = 0 if r == 0 else 16
        idx = np.array([[1 - r, r, hbase, hbase * 128, (16 - hbase) * 128,
                         (2 * s + hbase // 16) * 128, (1 - r) * 1024, r * 1024]], np.int32)
        gate = np.full((128, 1), float(r), np.float32)
        im = dict(shared)
        im.update({"x_fm": xs, "xhalo_fm": halo, "gate_col": gate, "idx": idx})
        ims.append(im)
    return ims


def kernel(**inputs):
    ims = _mk_inputs(inputs)
    nc, _ = build_program()
    res = run_bass_kernel_spmd(nc, ims, core_ids=list(range(8)))
    full = np.zeros((4, 2048, D), np.float32)
    for c in range(8):
        s, r = c // 2, c % 2
        full[s, r * T:(r + 1) * T, :] = np.asarray(res.results[c]["out_fm"]).T
    return full


# version 02 (final)
# speedup vs baseline: 1.0628x; 1.0628x over previous
import math
import numpy as np
import ml_dtypes
import concourse.bass as bass
import concourse.mybir as mybir
from concourse.bass_utils import run_bass_kernel_spmd
from concourse.expressions import RuntimeValue
from contextlib import ExitStack

F32 = mybir.dt.float32
BF16 = mybir.dt.bfloat16
I32 = mybir.dt.int32
AF = mybir.ActivationFunctionType
ALU = mybir.AluOpType
AX = mybir.AxisListType.X

D = 4096
T = 1024
TH = T + 3
NKD = D // 128
INTER = 8192
NH = 128
HDM = 64
NG = 8
NCH = 8
CONV_DIM = 10240
PROJ = 18560
DT_MIN, DT_MAX = 1e-3, 100.0
EPS = 1e-5
AHD = 128
AG = 8
KV_DIM = 2048
ASCALE = 1.0 / math.sqrt(AHD)
E_EXP = 8
R_LORA = 16
LORA_SCALE = 2.0

ENGS = ("tensor", "vector", "scalar", "gpsimd", "sync")


class Prog:
    def __init__(self, nc):
        self.nc = nc
        self.streams = {e: [] for e in ENGS}
        self.cnt = {}

    def sem(self, name):
        h = self.nc.alloc_semaphore(name=name)
        self.cnt[h.name] = 0
        return h

    def bump(self, sem, k=1):
        self.cnt[sem.name] += k
        return self.cnt[sem.name]

    def val(self, sem):
        return self.cnt[sem.name]

    def em(self, eng, fn):
        self.streams[eng].append(fn)

    def wait(self, eng, sem, v):
        if v > 0:
            self.em(eng, lambda E, s=sem, vv=v: E.wait_ge(s, vv))

    def barrier(self, bar_sem, engs=ENGS):
        tgt = self.bump(bar_sem, len(engs))
        for e in engs:
            def f(E, s=bar_sem, t=tgt):
                E.sem_inc(s, 1)
                E.wait_ge(s, t)
            self.em(e, f)

    def emit(self, block):
        @block.tensor
        def _(E):
            for fn in self.streams["tensor"]:
                fn(E)

        @block.vector
        def _(E):
            for fn in self.streams["vector"]:
                fn(E)

        @block.scalar
        def _(E):
            for fn in self.streams["scalar"]:
                fn(E)

        @block.gpsimd
        def _(E):
            for fn in self.streams["gpsimd"]:
                fn(E)

        @block.sync
        def _(E):
            for fn in self.streams["sync"]:
                fn(E)


def build_program(debug_taps=(), upto="all", ssd_parts=None, nhb=NG, hb_skip=()):
    nc = bass.Bass("TRN2", target_bir_lowering=False, debug=False)
    gctx = ExitStack()
    E_ = gctx.enter_context

    def din(name, shape, dt=F32):
        return nc.dram_tensor(name, list(shape), dt, kind="ExternalInput").ap()

    def dint(name, shape, dt):
        return nc.dram_tensor(name, list(shape), dt, kind="Internal").ap()

    x_in = din("x_fm", (D, T))
    xhalo_in = din("xhalo_fm", (D, 3))
    wt_inproj = din("wt_inproj", (D, PROJ))
    convw_in = din("convw", (128, 80 * 4))
    convb_in = din("convb", (128, 80))
    dtbias_in = din("dtbias_col", (128, 1))
    acol_in = din("a_col", (128, 1))
    drow_in = din("d_row", (1, 128))
    normw_in = din("normw_row", (1, INTER))
    wt_outproj = din("wt_outproj", (INTER, D))
    wq_in = din("wq", (D, D))
    wkv_in = din("wkv", (D, KV_DIM))
    wo_in = din("wo", (D, D))
    bo_in = din("bo_col", (128, NKD))
    wbase_in = din("wt_base", (D, D))
    bbase_in = din("bbase_col", (128, NKD))
    wrout_in = din("wt_router", (D, E_EXP))
    loraA_in = din("loraA_t", (D, 128))
    loraB_in = din("loraB_t", (128, D))
    e16_in = din("e16", (E_EXP, 128))
    ones_row_in = din("ones_row", (1, 128))
    ident_in = din("ident_f", (128, 128))
    tril_in = din("tril_sl", (128, 128))
    amask_in = din("attn_masks", (128, 4 * 512))
    gatecol_in = din("gate_col", (128, 1))
    idx_in = din("idx", (1, 8), I32)

    out_d = nc.dram_tensor("out_fm", [D, T], F32, kind="ExternalOutput").ap()

    r_dram = dint("r_dram", (D, TH), F32)
    xln_dram = dint("xln_dram", (D, TH), BF16)
    bc_dram = dint("bc_dram", (2 * NG * 128, T), BF16)
    gate_dram = dint("gate_dram", (T, INTER), BF16)
    y_dram = dint("y_dram", (T, INTER), BF16)
    yn_dram = dint("yn_dram", (INTER, T), BF16)
    cc1_in = dint("cc1_in", (128, NH * HDM), F32)
    cc1_out = dint("cc1_out", (256, NH * HDM), F32)
    q_cc = dint("q_cc", (D, T), BF16)
    k_cc = dint("k_cc", (AG * 128, T), BF16)
    v_cc = dint("v_cc", (T, AG * 128), BF16)
    q_g = dint("q_g", (2 * D, T), BF16)
    k_g = nc.dram_tensor("k_g", [8 * AG * 128, T], BF16, kind="Internal", addr_space="Shared").ap()
    v_g = nc.dram_tensor("v_g", [8 * T, AG * 128], BF16, kind="Internal", addr_space="Shared").ap()
    ctx_dram = dint("ctx_dram", (16 * 128, 2 * T), BF16)
    cc3_in = dint("cc3_in", (16 * 128, T), BF16)
    cc3_out = dint("cc3_out", (2 * 16 * 128, T), BF16)
    ctx_all = dint("ctx_all", (D, T), BF16)
    q_gg = dint("q_gg", (2 * 2048, T), BF16)
    k_gg = dint("k_gg", (8 * 128, T), BF16)
    v_gg = dint("v_gg", (8 * T, 128), BF16)

    dbg = {}

    def tap(name, ap):
        if name in debug_taps:
            o = nc.dram_tensor("dbg_" + name, list(ap.shape), ap.dtype,
                               kind="ExternalOutput").ap()
            dbg["dbg_" + name] = (o, ap)

    def tap_now(name, ap):
        if name in debug_taps:
            o = nc.dram_tensor("dbg_" + name, list(ap.shape), ap.dtype,
                               kind="ExternalOutput").ap()
            y_dma(o[:, :], ap[:, :], sem=s_st)
            dbg["dbg_" + name] = (o, None)

    ones_row = E_(nc.sbuf_tensor("sb_ones_row", [1, 128], F32))
    ones_col_b = E_(nc.sbuf_tensor("sb_ones_col_b", [128, 1], BF16))
    ident_f = E_(nc.sbuf_tensor("sb_ident_f", [128, 128], F32))
    ident_b = E_(nc.sbuf_tensor("sb_ident_b", [128, 128], BF16))
    tril_b = E_(nc.sbuf_tensor("sb_tril_b", [128, 128], BF16))
    convw = E_(nc.sbuf_tensor("sb_convw", [128, 320], F32))
    convb = E_(nc.sbuf_tensor("sb_convb", [128, 80], F32))
    dtbias = E_(nc.sbuf_tensor("sb_dtbias", [128, 1], F32))
    acol = E_(nc.sbuf_tensor("sb_acol", [128, 1], F32))
    drow = E_(nc.sbuf_tensor("sb_drow", [1, 128], F32))
    normw_row = E_(nc.sbuf_tensor("sb_normw", [1, 1024], F32))
    bbase = E_(nc.sbuf_tensor("sb_bbase", [128, NKD], F32))
    bocol = E_(nc.sbuf_tensor("sb_bocol", [128, NKD], F32))
    e16 = E_(nc.sbuf_tensor("sb_e16", [E_EXP, 128], F32))
    gatecol = E_(nc.sbuf_tensor("sb_gatecol", [128, 1], F32))
    idxs = E_(nc.sbuf_tensor("sb_idxs", [1, 8], I32))
    scrA = E_(nc.sbuf_tensor("scrA", [128, 3104], F32))
    scrB = E_(nc.sbuf_tensor("scrB", [128, 1024], F32))
    scrC = E_(nc.sbuf_tensor("scrC", [128, 4096], BF16))
    scrD = E_(nc.sbuf_tensor("scrD", [128, 3072], BF16))
    evac = E_(nc.sbuf_tensor("evac", [128, 1024], F32))
    rowA = E_(nc.sbuf_tensor("rowA", [1, 192], F32))
    rowB = E_(nc.sbuf_tensor("rowB", [1, 192], F32))
    colA = E_(nc.sbuf_tensor("colA", [128, 4], F32))

    pp = E_(nc.psum_tensor("pp", [128, 4096], F32))
    def bank(i, w=512):
        return pp[:, i * 512: i * 512 + w]
    ppb = pp[:, 7 * 512: 8 * 512].bitcast(BF16)

    P = Prog(nc)
    s_wld = P.sem("s_wld")
    s_xld = P.sem("s_xld")
    s_mm = P.sem("s_mm")
    s_ve = P.sem("s_ve")
    s_ac = P.sem("s_ac")
    s_st = P.sem("s_st")
    s_gst = P.sem("s_gst")
    s_cc = P.sem("s_cc")
    s_gp = P.sem("s_gp")
    s_bar = P.sem("s_bar")

    def g_dma(dst, src, sem=None):
        sem = sem or s_wld
        v = P.bump(sem, 16)
        P.em("gpsimd", lambda E, d=dst, s=src, ss=sem: E.dma_start(out=d, in_=s).then_inc(ss, 16))
        return v

    def y_dma(dst, src, sem=None):
        sem = sem or s_xld
        v = P.bump(sem, 16)
        P.em("sync", lambda E, d=dst, s=src, ss=sem: E.dma_start(out=d, in_=s).then_inc(ss, 16))
        return v

    def drain_all():
        P.wait("gpsimd", s_wld, P.val(s_wld))
        P.wait("gpsimd", s_gst, P.val(s_gst))
        P.wait("gpsimd", s_cc, P.val(s_cc))
        P.wait("sync", s_xld, P.val(s_xld))
        P.wait("sync", s_st, P.val(s_st))
        P.barrier(s_bar)

    def v_op(fn, n=1):
        v = P.bump(s_ve, n)
        def f(E, fn=fn, n=n):
            last = fn(E)
            while isinstance(last, (tuple, list)):
                last = last[-1]
            last.then_inc(s_ve, n)
        P.em("vector", f)
        return v

    def v_op1(fn):
        v = P.bump(s_ve)
        P.em("vector", lambda E, fn=fn: fn(E).then_inc(s_ve, 1))
        return v

    def v_chain(fns):
        v = 0
        for fn in fns:
            v = v_op1(fn)
            P.wait("vector", s_ve, v)
        return v

    def a_op1(fn):
        v = P.bump(s_ac)
        P.em("scalar", lambda E, fn=fn: fn(E).then_inc(s_ac, 1))
        return v

    def t_op(fn, n=1):
        v = P.bump(s_mm, n)
        def f(E, fn=fn, n=n):
            last = fn(E)
            last.then_inc(s_mm, n)
        P.em("tensor", f)
        return v

    def bcast16(slice_row, dst_bank_i):
        def f(E):
            r = slice_row.unsqueeze(2).broadcast_to([1, 16, 64])
            E.matmul(bank(dst_bank_i).rearrange("p (a b) -> p a b", a=8), ones_row[0:1, :],
                     r[:, 0:8, :], start=True, stop=True)
            return E.matmul(bank(dst_bank_i + 1).rearrange("p (a b) -> p a b", a=8), ones_row[0:1, :],
                            r[:, 8:16, :], start=True, stop=True)
        return t_op(f)


    def bct(tt):
        return tt.rearrange("p (a b) -> p a b", a=16)

    g_dma(ones_row[:, :], ones_row_in[:, :])
    g_dma(ones_col_b[:, :], ones_row_in.rearrange("a b -> b a"))
    g_dma(ident_f[:, :], ident_in[:, :])
    g_dma(ident_b[:, :], ident_in[:, :])
    g_dma(tril_b[:, :], tril_in[:, :])
    g_dma(convw[:, :], convw_in[:, :])
    g_dma(convb[:, :], convb_in[:, :])
    g_dma(dtbias[:, :], dtbias_in[:, :])
    g_dma(acol[:, :], acol_in[:, :])
    g_dma(drow[:, :], drow_in[:, :])
    g_dma(bbase[:, :], bbase_in[:, :])
    g_dma(bocol[:, :], bo_in[:, :])
    g_dma(e16[:, :], e16_in[:, :])
    g_dma(gatecol[:, :], gatecol_in[:, :])
    g_dma(idxs[:, :], idx_in[:, :])
    y_dma(r_dram[:, 0:3], xhalo_in[:, :])
    y_dma(r_dram[:, 3:TH], x_in[:, :])
    drain_all()

    def emit_ln(src, dst, cols, c_start=0):
        c0 = c_start
        while c0 < cols:
            w = min(64, cols - c0)
            srcv = src[:, c0:c0 + w].rearrange("(c p) t -> p c t", p=128)
            P.wait("sync", s_ve, P.val(s_ve))
            vld = y_dma(scrA[:, 0:NKD * w].rearrange("p (c t) -> p c t", c=NKD), srcv)
            P.wait("vector", s_xld, vld)
            vv = v_op1(lambda E, w=w: E.tensor_copy(scrC[:, 0:NKD * w], scrA[:, 0:NKD * w]))
            P.wait("scalar", s_ve, vv)
            P.wait("scalar", s_st, P.val(s_st))
            va = a_op1(lambda E, w=w: E.activation(scrD[:, 0:NKD * w], scrC[:, 0:NKD * w], AF.Square))
            P.wait("tensor", s_ve, vv)
            P.wait("tensor", s_ac, va)
            def f_stats(E, w=w):
                for k in range(NKD):
                    E.matmul(bank(0)[0:1, 0:w], ones_col_b[:, 0:1], scrC[:, k * w:(k + 1) * w],
                             start=(k == 0), stop=(k == NKD - 1))
                last = None
                for k in range(NKD):
                    last = E.matmul(bank(1)[0:1, 0:w], ones_col_b[:, 0:1], scrD[:, k * w:(k + 1) * w],
                                    start=(k == 0), stop=(k == NKD - 1))
                return last
            vm = t_op(f_stats)
            P.wait("vector", s_mm, vm)
            vv2 = v_chain([
                lambda E, w=w: E.tensor_scalar_mul(rowA[0:1, 0:w], bank(0)[0:1, 0:w], 1.0 / D),
                lambda E, w=w: E.tensor_scalar_mul(rowB[0:1, 0:w], bank(1)[0:1, 0:w], 1.0 / D),
                lambda E, w=w: E.tensor_mul(rowA[0:1, 128:128 + w], rowA[0:1, 0:w], rowA[0:1, 0:w]),
                lambda E, w=w: E.tensor_sub(rowB[0:1, 0:w], rowB[0:1, 0:w], rowA[0:1, 128:128 + w]),
                lambda E, w=w: E.tensor_scalar_add(rowB[0:1, 0:w], rowB[0:1, 0:w], EPS)])
            P.wait("scalar", s_ve, vv2)
            a_op1(lambda E, w=w: E.activation(rowB[0:1, 128:128 + w], rowB[0:1, 0:w], AF.Ln))
            va2 = a_op1(lambda E, w=w: E.activation(rowB[0:1, 0:w], rowB[0:1, 128:128 + w],
                                                    AF.Exp, scale=-0.5))
            P.wait("vector", s_ac, va2)
            vv3 = v_op(lambda E, w=w: (
                E.scalar_tensor_tensor(rowA[0:1, 0:w], rowA[0:1, 0:w], -1.0, rowB[0:1, 0:w],
                                       ALU.mult, ALU.mult),))
            P.wait("tensor", s_ve, vv3)
            def f_bc(E, w=w):
                E.matmul(bank(2)[:, 0:w], ones_row[0:1, :], rowB[0:1, 0:w], start=True, stop=True)
                return E.matmul(bank(3)[:, 0:w], ones_row[0:1, :], rowA[0:1, 0:w],
                                start=True, stop=True)
            vm2 = t_op(f_bc)
            P.wait("vector", s_mm, vm2)
            v_op(lambda E, w=w: (
                E.tensor_copy(scrB[:, 0:w], bank(2)[:, 0:w]),
                E.tensor_copy(scrB[:, 512:512 + w], bank(3)[:, 0:w])))
            vv5 = v_op(lambda E, w=w: [
                (E.tensor_mul(scrA[:, k * w:(k + 1) * w], scrA[:, k * w:(k + 1) * w], scrB[:, 0:w]),
                 E.tensor_add(scrD[:, k * w:(k + 1) * w], scrA[:, k * w:(k + 1) * w], scrB[:, 512:512 + w]))
                for k in range(NKD)])
            P.wait("sync", s_ve, vv5)
            dstv = dst[:, c0:c0 + w].rearrange("(c p) t -> p c t", p=128)
            y_dma(dstv, scrD[:, 0:NKD * w].rearrange("p (c t) -> p c t", c=NKD), sem=s_st)
            c0 += w
        P.wait("sync", s_st, P.val(s_st))

    emit_ln(r_dram, xln_dram, TH)
    drain_all()
    tap("xln1", xln_dram)
    if upto == "p1":
        _finish(P, dbg, y_dma, out_d, r_dram, s_st, s_xld)
        with nc.Block() as block:
            P.emit(block)
        gctx.close()
        return nc, dbg

    mctx = ExitStack()
    M_ = mctx.enter_context
    xln_sb = M_(nc.sbuf_tensor("xln_sb", [128, NKD * TH], BF16))
    wbuf = [M_(nc.sbuf_tensor(f"wbuf{i}", [128, NKD * 128], BF16)) for i in range(2)]
    carry = M_(nc.sbuf_tensor("carry", [128, NH * HDM], F32))
    dtproc = M_(nc.sbuf_tensor("dtproc", [128, T], F32))
    cs = M_(nc.sbuf_tensor("cs", [128, T], F32))
    csT = M_(nc.sbuf_tensor("csT", [128, T], F32))
    dtT = M_(nc.sbuf_tensor("dtT", [128, T], BF16))
    dcyT = M_(nc.sbuf_tensor("dcyT", [128, T], BF16))
    expcsT = M_(nc.sbuf_tensor("expcsT", [128, T], F32))
    bcg = M_(nc.sbuf_tensor("bcg", [128, 2 * T], BF16))
    xfm = M_(nc.sbuf_tensor("xfm", [128, 8 * T], BF16))
    xbcraw = M_(nc.sbuf_tensor("xbcraw", [128, TH + 1], BF16))
    cdrow = M_(nc.sbuf_tensor("cdrow", [1, T], F32))
    cumdrow = M_(nc.sbuf_tensor("cumdrow", [1, T], F32))
    cdcol = M_(nc.sbuf_tensor("cdcol", [128, NCH], F32))
    cumdec = M_(nc.sbuf_tensor("cumdec", [128, NCH], F32))
    tr8 = M_(nc.sbuf_tensor("tr8", [8, 128], F32))

    vxln = y_dma(xln_sb[:, :].rearrange("p (c t) -> p c t", c=NKD),
                 xln_dram[:, :].rearrange("(c p) t -> p c t", p=128))
    P.wait("tensor", s_xld, vxln)

    TBS = [(0, 512), (512, 512), (1024, 3)]
    pcs = {"bank": 0, "bankv": [0] * 6, "buf": 0, "bufv": [0, 0]}

    def load_piece(col0, width):
        b = pcs["buf"]
        pcs["buf"] ^= 1
        P.wait("gpsimd", s_mm, pcs["bufv"][b])
        v = g_dma(wbuf[b][:, 0:NKD * width].rearrange("p (c m) -> p c m", c=NKD),
                  wt_inproj[:, col0:col0 + width].rearrange("(c p) m -> p c m", p=128))
        return b, v

    def mm_piece_fm(b, vld, width, evac_fn):
        P.wait("tensor", s_wld, vld)
        nfc = width // 128
        for tb0, w in TBS:
            for fc in range(nfc):
                bk = pcs["bank"]
                pcs["bank"] = (pcs["bank"] + 1) % 6
                P.wait("tensor", s_ve, pcs["bankv"][bk])
                def f_mm(E, b=b, fc=fc, tb0=tb0, w=w, bk=bk, width=width):
                    last = None
                    for k in range(NKD):
                        last = E.matmul(bank(bk)[:, 0:w],
                                        wbuf[b][:, k * width + fc * 128: k * width + fc * 128 + 128],
                                        xln_sb[:, k * TH + tb0: k * TH + tb0 + w],
                                        start=(k == 0), stop=(k == NKD - 1))
                    return last
                vm = t_op(f_mm)
                P.wait("vector", s_mm, vm)
                pcs["bankv"][bk] = evac_fn(fc, tb0, w, bk)
        pcs["bufv"][pcs["buf"] ^ 1] = P.val(s_mm)

    def mm_piece_gate(b, vld, width, hb, j):
        P.wait("tensor", s_wld, vld)
        for tc in range(8):
            bk = pcs["bank"]
            pcs["bank"] = (pcs["bank"] + 1) % 6
            P.wait("tensor", s_ve, pcs["bankv"][bk])
            def f_mm(E, b=b, tc=tc, bk=bk, width=width):
                last = None
                for k in range(NKD):
                    last = E.matmul(bank(bk)[:, 0:width],
                                    xln_sb[:, k * TH + 3 + tc * 128: k * TH + 3 + tc * 128 + 128],
                                    wbuf[b][:, k * width: (k + 1) * width],
                                    start=(k == 0), stop=(k == NKD - 1))
                return last
            vm = t_op(f_mm)
            P.wait("vector", s_mm, vm)
            slot = tc % 2
            P.wait("vector", s_st, max(0, P.val(s_st) - 16))
            vv = v_op1(lambda E, bk=bk, slot=slot, width=width:
                       E.tensor_copy(scrD[:, slot * 1024: slot * 1024 + width], bank(bk)[:, 0:width]))
            pcs["bankv"][bk] = vv
            P.wait("sync", s_ve, vv)
            y_dma(gate_dram[tc * 128:(tc + 1) * 128, hb * 1024 + j * 128: hb * 1024 + j * 128 + width],
                  scrD[:, slot * 1024: slot * 1024 + width], sem=s_st)
        pcs["bufv"][pcs["buf"] ^ 1] = P.val(s_mm)

    def conv_silu(raw_ap, conv_chunk, dst_ap):
        P.wait("vector", s_ac, P.val(s_ac))
        vv = v_op(lambda E, r=raw_ap, ch=conv_chunk: (
            E.tensor_scalar_mul(scrB[:, 0:T], r[:, 0:T], convw[:, ch * 4:ch * 4 + 1]),
            [E.scalar_tensor_tensor(scrB[:, 0:T], r[:, kk:kk + T],
                                    convw[:, ch * 4 + kk:ch * 4 + kk + 1],
                                    scrB[:, 0:T], ALU.mult, ALU.add) for kk in range(1, 4)]))
        P.wait("scalar", s_ve, vv)
        va = a_op1(lambda E, ch=conv_chunk, d=dst_ap: E.activation(
            d, scrB[:, 0:T], AF.Silu, bias=convb[:, ch:ch + 1]))
        return va

    for pi in range(16):
        col0 = 2 * INTER + pi * 128
        b, vld = load_piece(col0, 128)

        def evac_bc(fc, tb0, w, bk):
            return v_op1(lambda E, tb0=tb0, w=w, bk=bk:
                         E.tensor_copy(xbcraw[:, tb0: tb0 + w], bank(bk)[:, 0:w]))
        mm_piece_fm(b, vld, 128, evac_bc)
        gch = 64 + pi
        rowd = pi * 128
        P.wait("scalar", s_st, max(0, P.val(s_st) - 16))
        va = conv_silu(xbcraw[:, 0:TH], gch, scrD[:, (pi % 2) * T:(pi % 2) * T + T])
        P.wait("sync", s_ac, va)
        y_dma(bc_dram[rowd:rowd + 128, :], scrD[:, (pi % 2) * T:(pi % 2) * T + T], sem=s_st)

    b, vld = load_piece(2 * INTER + 2048, 128)

    def evac_dt(fc, tb0, w, bk):
        return v_op1(lambda E, tb0=tb0, w=w, bk=bk:
                     E.tensor_copy(scrA[:, tb0:tb0 + w], bank(bk)[:, 0:w]))
    mm_piece_fm(b, vld, 128, evac_dt)

    P.wait("scalar", s_ve, P.val(s_ve))
    a_op1(lambda E: E.activation(scrA[:, 1030:1030 + T], scrA[:, 3:3 + T], AF.Abs, bias=dtbias[:, 0:1]))
    a_op1(lambda E: E.activation(scrA[:, 1030:1030 + T], scrA[:, 1030:1030 + T], AF.Exp, scale=-1.0))
    a_op1(lambda E: E.activation(scrA[:, 1030:1030 + T], scrA[:, 1030:1030 + T], AF.Ln, bias=1.0))
    a4 = a_op1(lambda E: E.activation(scrA[:, 2054:2054 + T], scrA[:, 3:3 + T], AF.Relu, bias=dtbias[:, 0:1]))
    P.wait("vector", s_ac, a4)
    v_op(lambda E: (
        E.tensor_add(dtproc[:, :], scrA[:, 2054:2054 + T], scrA[:, 1030:1030 + T]),
        E.tensor_scalar(dtproc[:, :], dtproc[:, :], DT_MIN, DT_MAX, ALU.max, ALU.min),
        E.tensor_scalar_mul(scrB[:, 0:T], dtproc[:, :], acol[:, 0:1])))
    vv = v_op(lambda E: [E.tensor_tensor_scan(cs[:, c * 128:(c + 1) * 128],
                                              scrB[:, c * 128:(c + 1) * 128],
                                              scrB[:, c * 128:(c + 1) * 128],
                                              0.0, ALU.add, ALU.bypass) for c in range(NCH)])
    P.wait("scalar", s_ve, vv)
    va = 0
    for c in range(NCH):
        va = a_op1(lambda E, c=c: E.activation(cdcol[:, c:c + 1], cs[:, c * 128 + 127:c * 128 + 128], AF.Exp))
    P.wait("vector", s_ac, va)
    v_chain([lambda E: E.memset(cumdec[:, 0:1], 1.0)] +
            [(lambda E, c=c: E.tensor_mul(cumdec[:, c:c + 1], cumdec[:, c - 1:c], cdcol[:, c - 1:c]))
             for c in range(1, NCH)])
    for c in range(NCH):
        P.wait("scalar", s_mm, P.val(s_mm))
        va = a_op1(lambda E, c=c: E.activation(scrA[:, 0:128], cs[:, c * 128:(c + 1) * 128], AF.Exp,
                                               scale=-1.0, bias=cs[:, c * 128 + 127:c * 128 + 128]))
        va = a_op1(lambda E, c=c: E.activation(scrA[:, 128:256], cs[:, c * 128:(c + 1) * 128], AF.Exp))
        P.wait("tensor", s_ac, va)
        P.wait("tensor", s_ve, P.val(s_ve))
        def f_tr(E, c=c):
            E.transpose(bank(0)[:, 0:128], cs[:, c * 128:(c + 1) * 128], ident_f[:, :])
            E.transpose(bank(0)[:, 128:256], dtproc[:, c * 128:(c + 1) * 128], ident_f[:, :])
            E.transpose(bank(1)[:, 0:128], scrA[:, 0:128], ident_f[:, :])
            return E.transpose(bank(1)[:, 128:256], scrA[:, 128:256], ident_f[:, :])
        vm = t_op(f_tr)
        P.wait("vector", s_mm, vm)
        v_op(lambda E, c=c: (
            E.tensor_copy(csT[:, c * 128:(c + 1) * 128], bank(0)[:, 0:128]),
            E.tensor_copy(dtT[:, c * 128:(c + 1) * 128], bank(0)[:, 128:256]),
            E.tensor_copy(dcyT[:, c * 128:(c + 1) * 128], bank(1)[:, 0:128]),
            E.tensor_copy(expcsT[:, c * 128:(c + 1) * 128], bank(1)[:, 128:256])))
    P.wait("tensor", s_ve, P.val(s_ve))
    vm = t_op(lambda E: (E.transpose(bank(0)[0:8, 0:128], cdcol[:, 0:NCH], ident_f[:, :]),
                         E.transpose(bank(1)[0:8, 0:128], cumdec[:, 0:NCH], ident_f[:, :]))[-1])
    P.wait("vector", s_mm, vm)
    vv = v_op1(lambda E: E.tensor_copy(tr8[0:8, 0:128], bank(0)[0:8, 0:128]))
    P.wait("sync", s_ve, vv)
    vst = y_dma(cdrow[0:1, :], tr8[0:8, 0:128], sem=s_st)
    P.wait("vector", s_st, vst)
    vv = v_op1(lambda E: E.tensor_copy(tr8[0:8, 0:128], bank(1)[0:8, 0:128]))
    P.wait("sync", s_ve, vv)
    y_dma(cumdrow[0:1, :], tr8[0:8, 0:128], sem=s_st)
    v_op1(lambda E: E.memset(carry[:, :], 0.0))
    drain_all()
    tap("bc", bc_dram)
    tap("dt", dtproc)
    tap("cs", cs)
    tap("gate", gate_dram)
    tap("ydram", y_dram)

    if upto == "p2":
        _finish(P, dbg, y_dma, out_d, r_dram, s_st, s_xld)
        with nc.Block() as block:
            P.emit(block)
        mctx.close()
        gctx.close()
        return nc, dbg

    pA = pp[:, 0:1024]
    pB = pp[:, 1024:2048]
    pC = pp[:, 2048:3072]
    pS = pp[:, 3072:3584]

    for hb in range(nhb):
        for j in range(8 if ("x", hb) not in hb_skip else 0):
            col0 = INTER + hb * 1024 + j * 128
            b, vld = load_piece(col0, 128)

            def evac_x(fc, tb0, w, bk):
                return v_op1(lambda E, tb0=tb0, w=w, bk=bk:
                             E.tensor_copy(xbcraw[:, tb0: tb0 + w], bank(bk)[:, 0:w]))
            mm_piece_fm(b, vld, 128, evac_x)
            gch = hb * 8 + j
            conv_silu(xbcraw[:, 0:TH], gch, xfm[:, j * T:(j + 1) * T])
        for j in range(8 if ("gate", hb) not in hb_skip else 0):
            col0 = hb * 1024 + j * 128
            b, vld = load_piece(col0, 128)
            mm_piece_gate(b, vld, 128, hb, j)
        if ("bcg", hb) not in hb_skip:
            P.wait("sync", s_st, P.val(s_st))
            P.wait("sync", s_mm, P.val(s_mm))
            vbc = y_dma(bcg[:, 0:T], bc_dram[hb * 128:(hb + 1) * 128, :])
            vbc = y_dma(bcg[:, T:2 * T], bc_dram[1024 + hb * 128: 1024 + (hb + 1) * 128, :])
            P.wait("tensor", s_xld, vbc)
        if ("drow", hb) not in hb_skip:
            P.wait("tensor", s_ac, P.val(s_ac))
            P.wait("tensor", s_ve, P.val(s_ve))
            vm = bcast16(drow[0:1, 16 * hb:16 * hb + 16], 0)
            P.wait("vector", s_mm, vm)
            vdr = v_op(lambda E: (E.tensor_copy(scrA[:, 0:512], bank(0)),
                                  E.tensor_copy(scrA[:, 512:1024], bank(1))))

        for c in range(NCH if ("ssd", hb) not in hb_skip else 0):
            hsl = slice(c * 128 + 16 * hb, c * 128 + 16 * hb + 16)

            P.wait("tensor", s_ve, P.val(s_ve))
            vm = t_op(lambda E, c=c: E.matmul(pS[:, 0:128], bcg[:, c * 128:(c + 1) * 128],
                                              bcg[:, T + c * 128: T + c * 128 + 128],
                                              start=True, stop=True))
            P.wait("vector", s_mm, vm)
            vms = v_op1(lambda E: E.tensor_tensor(scrD[:, 0:128], pS[:, 0:128], tril_b[:, :], ALU.mult))

            def f_xtr(E, c=c):
                last = None
                for fc in range(8):
                    last = E.transpose(ppb[:, fc * 128:(fc + 1) * 128],
                                       xfm[:, fc * T + c * 128: fc * T + c * 128 + 128], ident_b[:, :])
                return last
            vm = t_op(f_xtr)
            P.wait("vector", s_mm, vm)
            vxt = v_op1(lambda E: E.tensor_copy(scrC[:, 0:1024], ppb[:, 0:1024]))
            vxd = v_op(lambda E, c=c, hb=hb: (
                E.tensor_tensor(bct(scrC[:, 1024:2048]), bct(scrC[:, 0:1024]),
                                dtT[:, c * 128 + 16 * hb: c * 128 + 16 * hb + 16]
                                .unsqueeze(2).broadcast_to([128, 16, 64]), ALU.mult),
                E.tensor_tensor(bct(scrC[:, 2048:3072]), bct(scrC[:, 1024:2048]),
                                dcyT[:, c * 128 + 16 * hb: c * 128 + 16 * hb + 16]
                                .unsqueeze(2).broadcast_to([128, 16, 64]), ALU.mult),
                E.tensor_mul(scrC[:, 3072:4096], scrC[:, 0:1024], scrA[:, 0:1024])), n=3)
            P.wait("tensor", s_ve, vxt)
            vm = t_op(lambda E, c=c: E.transpose(ppb[:, 0:128], bcg[:, c * 128:(c + 1) * 128],
                                                 ident_b[:, :]))
            P.wait("vector", s_mm, vm)
            vbt = v_op1(lambda E: E.tensor_copy(scrD[:, 128:256], ppb[:, 0:128]))
            vcb = v_op1(lambda E, hb=hb: E.tensor_copy(scrD[:, 1024:2048], carry[:, hb * 1024:(hb + 1) * 1024]))

            P.wait("tensor", s_ve, max(vbt, vcb, vxd))
            def f_big(E, c=c):
                E.matmul(pC[:, 0:512], scrD[:, 128:256], scrC[:, 2048:2560], start=True, stop=True)
                E.matmul(pC[:, 512:1024], scrD[:, 128:256], scrC[:, 2560:3072], start=True, stop=True)
                E.matmul(pB[:, 0:512], bcg[:, T + c * 128: T + c * 128 + 128], scrD[:, 1024:1536],
                         start=True, stop=True)
                E.matmul(pB[:, 512:1024], bcg[:, T + c * 128: T + c * 128 + 128], scrD[:, 1536:2048],
                         start=True, stop=True)
                if ssd_parts != "YD":
                    E.matmul(pA[:, 0:512], ident_b[:, :], scrC[:, 3072:3584], start=True, stop=True)
                    return E.matmul(pA[:, 512:1024], ident_b[:, :], scrC[:, 3584:4096], start=True, stop=True)
                return E.matmul(pB[:, 0:512], bcg[:, T + c * 128: T + c * 128 + 128], scrD[:, 1024:1536],
                                start=True, stop=True)
            vmbig = t_op(f_big)

            P.wait("vector", s_ve, vms)
            segps = [pS[:, 128:256], pp[:, 7 * 512 + 128: 7 * 512 + 256]]
            sega = [scrA[:, 1030:1158], scrA[:, 1158:1286]]
            lexd = [scrD[:, 256:384], scrD[:, 512:640]]
            mtd = [scrD[:, 384:512], scrD[:, 640:768]]
            vsg_l, vle_l, vmt_l, vseg_l = {}, {}, {}, {}
            P.wait("tensor", s_ve, P.val(s_ve))
            for step in range(18):
                j3 = step - 2
                if 0 <= j3 < 16 and ssd_parts != "D":
                    P.wait("tensor", s_ve, vmt_l[j3])
                    t_op(lambda E, jh=j3, p=j3 % 2: E.matmul(
                        pA[:, jh * 64:(jh + 1) * 64], mtd[p],
                        scrC[:, 1024 + jh * 64: 1024 + (jh + 1) * 64],
                        start=(ssd_parts == "YD"), stop=(jh == 15)))
                jh = step
                if jh < 16:
                    gh = 16 * hb + jh
                    p = jh % 2
                    if jh >= 2:
                        P.wait("tensor", s_ve, vsg_l[jh - 2])
                    vseg_l[jh] = t_op(lambda E, c=c, gh=gh, p=p: E.matmul(
                        segps[p],
                        ident_f[:, gh:gh + 1].broadcast_to([128, 128]),
                        cs[:, c * 128:(c + 1) * 128], start=True, stop=True))
                j2 = step - 1
                if 0 <= j2 < 16:
                    gh2 = 16 * hb + j2
                    p2 = j2 % 2
                    P.wait("vector", s_mm, vseg_l[j2])
                    vsg_l[j2] = v_op1(lambda E, c=c, gh=gh2, p=p2: E.tensor_scalar(
                        sega[p], segps[p], csT[:, c * 128 + gh:c * 128 + gh + 1],
                        0.0, ALU.subtract, ALU.min))
                    P.wait("scalar", s_ve, vsg_l[j2])
                    vle_l[j2] = a_op1(lambda E, p=p2: E.activation(lexd[p], sega[p], AF.Exp))
                    P.wait("vector", s_ac, vle_l[j2])
                    vmt_l[j2] = v_op1(lambda E, p=p2: E.tensor_tensor(mtd[p], scrD[:, 0:128],
                                                                     lexd[p], ALU.mult))
            vmall = P.val(s_mm)
            P.wait("vector", s_mm, vmall)
            P.wait("vector", s_gst, P.val(s_gst))
            if ssd_parts == "A":
                ve1 = v_op(lambda E, c=c, hb=hb: (
                    E.tensor_copy(scrB[:, 0:1024], pA[:, 0:1024]),
                    E.tensor_copy(colA[:, 0:1], colA[:, 0:1])), n=2)
            elif ssd_parts == "B":
                ve1 = v_op(lambda E, c=c, hb=hb: (
                    E.tensor_tensor(bct(scrB[:, 0:1024]), bct(pB[:, 0:1024]),
                                    expcsT[:, c * 128 + 16 * hb: c * 128 + 16 * hb + 16]
                                    .unsqueeze(2).broadcast_to([128, 16, 64]), ALU.mult),
                    E.tensor_copy(colA[:, 0:1], colA[:, 0:1])), n=2)
            else:
                ve1 = v_op(lambda E, c=c, hb=hb: (
                    E.tensor_tensor(bct(scrB[:, 0:1024]), bct(pB[:, 0:1024]),
                                    expcsT[:, c * 128 + 16 * hb: c * 128 + 16 * hb + 16]
                                    .unsqueeze(2).broadcast_to([128, 16, 64]), ALU.mult),
                    E.tensor_add(scrB[:, 0:1024], scrB[:, 0:1024], pA[:, 0:1024])), n=2)
            P.wait("tensor", s_ve, ve1)
            vm = bcast16(cdrow[0:1, c * 128 + 16 * hb: c * 128 + 16 * hb + 16], 0)
            P.wait("vector", s_mm, vm)
            v_op(lambda E, hb=hb: (
                E.tensor_tensor(carry[:, hb * 1024:(hb + 1) * 1024],
                                carry[:, hb * 1024:(hb + 1) * 1024], pA[:, 0:1024], ALU.mult),
                E.tensor_tensor(carry[:, hb * 1024:(hb + 1) * 1024],
                                carry[:, hb * 1024:(hb + 1) * 1024], pC[:, 0:1024], ALU.add)), n=2)
            P.wait("gpsimd", s_ve, ve1)
            g_dma(y_dram[c * 128:(c + 1) * 128, hb * 1024:(hb + 1) * 1024], scrB[:, 0:1024], sem=s_gst)

    if "xfm" in debug_taps:
        xfm_dbg = nc.dram_tensor("dbg_xfm", [128, 8 * T], BF16, kind="ExternalOutput").ap()
        dbg["dbg_xfm"] = (None, None)
        y_dma(xfm_dbg[:, :], xfm[:, :], sem=s_st)
        dbg.pop("dbg_xfm")
    drain_all()
    if upto == "p4":
        _finish(P, dbg, y_dma, out_d, r_dram, s_st, s_xld)
        with nc.Block() as block:
            P.emit(block)
        mctx.close()
        gctx.close()
        return nc, dbg

    y_dma(cc1_in[:, :], carry[:, :], sem=s_st)
    P.wait("gpsimd", s_st, P.val(s_st))
    vcc = P.bump(s_cc)
    P.em("gpsimd", lambda E: E.collective_compute(
        "AllGather", ALU.bypass,
        replica_groups=[[0, 1], [2, 3], [4, 5], [6, 7]],
        ins=[cc1_in[:, :]], outs=[cc1_out[:, :]]).then_inc(s_cc, 1))
    P.wait("gpsimd", s_cc, vcc)

    def g_dma_dyn(dst, srcfn):
        v = P.bump(s_wld, 16)
        def f(E, d=dst, sf=srcfn):
            with E.register("rpart") as r:
                E.reg_load(r, idxs[0:1, 0:1])
                rv = RuntimeValue(val=r, min_val=0, max_val=1)
                E.dma_start(out=d, in_=sf(rv)).then_inc(s_wld, 16)
        P.em("gpsimd", f)
        return v

    vin = g_dma_dyn(carry[:, :], lambda rv: cc1_out[bass.ds(rv * 128, 128), :])
    P.wait("vector", s_wld, vin)
    v_op1(lambda E: E.tensor_scalar_mul(carry[:, :], carry[:, :], gatecol[:, 0:1]))
    drain_all()

    for hb in range(NG):
        P.wait("sync", s_mm, P.val(s_mm))
        vbc = y_dma(bcg[:, 0:T], bc_dram[hb * 128:(hb + 1) * 128, :])
        vbc = y_dma(bcg[:, T:2 * T], bc_dram[1024 + hb * 128: 1024 + (hb + 1) * 128, :])
        P.wait("tensor", s_xld, vbc)
        P.wait("sync", s_mm, P.val(s_mm))
        vnwl = y_dma(normw_row[0:1, :], normw_in[0:1, hb * 1024:(hb + 1) * 1024])
        P.wait("tensor", s_xld, vnwl)
        def f_nw(E, hb=hb):
            r = normw_row[0:1, 0:1024]
            E.matmul(bank(4), ones_row[0:1, :], r[0:1, 0:512], start=True, stop=True)
            return E.matmul(bank(5), ones_row[0:1, :], r[0:1, 512:1024], start=True, stop=True)
        P.wait("tensor", s_ve, P.val(s_ve))
        vm = t_op(f_nw)
        P.wait("vector", s_mm, vm)
        vnw = v_op(lambda E: (E.tensor_copy(evac[:, 0:512], bank(4)),
                              E.tensor_copy(evac[:, 512:1024], bank(5))))
        for c in range(NCH):
            P.wait("tensor", s_ve, P.val(s_ve))
            vm = bcast16(cumdrow[0:1, c * 128 + 16 * hb: c * 128 + 16 * hb + 16], 0)
            P.wait("vector", s_mm, vm)
            vdp = v_op1(lambda E, hb=hb: E.tensor_tensor(scrD[:, 1024:2048],
                                                         carry[:, hb * 1024:(hb + 1) * 1024],
                                                         pA[:, 0:1024], ALU.mult))
            P.wait("tensor", s_ve, vdp)
            vm = t_op(lambda E, c=c: (
                E.matmul(pB[:, 0:512], bcg[:, T + c * 128: T + c * 128 + 128], scrD[:, 1024:1536],
                         start=True, stop=True),
                E.matmul(pB[:, 512:1024], bcg[:, T + c * 128: T + c * 128 + 128], scrD[:, 1536:2048],
                         start=True, stop=True))[-1])
            P.wait("sync", s_ve, P.val(s_ve))
            vy = y_dma(scrC[:, 0:1024], y_dram[c * 128:(c + 1) * 128, hb * 1024:(hb + 1) * 1024])
            vg = y_dma(scrC[:, 1024:2048], gate_dram[c * 128:(c + 1) * 128, hb * 1024:(hb + 1) * 1024])
            P.wait("scalar", s_xld, vg)
            vsl = a_op1(lambda E: E.activation(scrD[:, 0:1024], scrC[:, 1024:2048], AF.Silu))
            P.wait("vector", s_mm, vm)
            P.wait("vector", s_xld, vy)
            P.wait("vector", s_ac, vsl)
            vyf = v_op(lambda E, c=c, hb=hb: (
                E.tensor_tensor(bct(scrB[:, 0:1024]), bct(pB[:, 0:1024]),
                                expcsT[:, c * 128 + 16 * hb: c * 128 + 16 * hb + 16]
                                .unsqueeze(2).broadcast_to([128, 16, 64]), ALU.mult),
                E.tensor_add(scrB[:, 0:1024], scrB[:, 0:1024], scrC[:, 0:1024]),
                E.tensor_mul(scrB[:, 0:1024], scrB[:, 0:1024], scrD[:, 0:1024])), n=3)
            P.wait("scalar", s_ve, vyf)
            vsq = a_op1(lambda E: E.activation(scrD[:, 2048:3072], scrB[:, 0:1024], AF.Square,
                                               accum_out=colA[:, 0:1]))
            P.wait("vector", s_ac, vsq)
            vrs = v_op1(lambda E: E.tensor_scalar(colA[:, 1:2], colA[:, 0:1], 1.0 / 1024.0, EPS,
                                                  ALU.mult, ALU.add))
            P.wait("scalar", s_ve, vrs)
            a_op1(lambda E: E.activation(colA[:, 2:3], colA[:, 1:2], AF.Ln))
            vsq2 = a_op1(lambda E: E.activation(colA[:, 3:4], colA[:, 2:3], AF.Exp, scale=-0.5))
            P.wait("vector", s_ac, vsq2)
            vyn = v_op1(lambda E: E.scalar_tensor_tensor(scrC[:, 2048:3072], scrB[:, 0:1024],
                                                         colA[:, 3:4], evac[:, 0:1024],
                                                         ALU.mult, ALU.mult))
            P.wait("tensor", s_ve, vyn)
            def f_ytr(E):
                last = None
                for fc in range(8):
                    last = E.transpose(ppb[:, fc * 128:(fc + 1) * 128],
                                       scrC[:, 2048 + fc * 128: 2048 + (fc + 1) * 128], ident_b[:, :])
                return last
            vm2 = t_op(f_ytr)
            P.wait("vector", s_mm, vm2)
            P.wait("vector", s_st, P.val(s_st))
            vev = v_op1(lambda E: E.tensor_copy(scrC[:, 3072:4096], ppb[:, 0:1024]))
            P.wait("sync", s_ve, vev)
            y_dma(yn_dram[hb * 1024:(hb + 1) * 1024, c * 128:(c + 1) * 128]
                  .rearrange("(f p) t -> p f t", p=128),
                  scrC[:, 3072:4096].rearrange("p (f t) -> p f t", f=8), sem=s_st)
    drain_all()
    tap("yn", yn_dram)
    mctx.close()

    if upto == "p6":
        _finish(P, dbg, y_dma, out_d, r_dram, s_st, s_xld)
        with nc.Block() as block:
            P.emit(block)
        gctx.close()
        return nc, dbg

    octx = ExitStack()
    O_ = octx.enter_context
    NKY = INTER // 128
    yrhs = O_(nc.sbuf_tensor("yrhs", [128, NKY * 512], BF16))
    owbuf = [O_(nc.sbuf_tensor(f"owbuf{i}", [128, NKY * 256], BF16)) for i in range(2)]
    ost = {"buf": 0, "bufv": [0, 0], "bank": 0, "bankv": [0] * 8}

    def oload(col0):
        b = ost["buf"]
        ost["buf"] ^= 1
        P.wait("gpsimd", s_mm, ost["bufv"][b])
        v = g_dma(owbuf[b][:, :].rearrange("p (c m) -> p c m", c=NKY),
                  wt_outproj[:, col0:col0 + 256].rearrange("(c p) m -> p c m", p=128))
        return b, v

    for tb in range(2):
        P.wait("sync", s_mm, P.val(s_mm))
        vy = y_dma(yrhs[:, :].rearrange("p (c t) -> p c t", c=NKY),
                   yn_dram[:, tb * 512:(tb + 1) * 512].rearrange("(c p) t -> p c t", p=128))
        P.wait("tensor", s_xld, vy)
        for pi in range(16):
            b, vld = oload(pi * 256)
            P.wait("tensor", s_wld, vld)
            for mb in range(2):
                bk = ost["bank"]
                ost["bank"] = (ost["bank"] + 1) % 8
                P.wait("tensor", s_ve, ost["bankv"][bk])
                def f_mm(E, b=b, mb=mb, bk=bk):
                    last = None
                    for k in range(NKY):
                        last = E.matmul(bank(bk),
                                        owbuf[b][:, k * 256 + mb * 128: k * 256 + mb * 128 + 128],
                                        yrhs[:, k * 512:(k + 1) * 512],
                                        start=(k == 0), stop=(k == NKY - 1))
                    return last
                vm = t_op(f_mm)
                row = pi * 256 + mb * 128
                P.wait("sync", s_ve, P.val(s_ve))
                vr = y_dma(scrA[:, 0:512], r_dram[row:row + 128, 3 + tb * 512: 3 + (tb + 1) * 512])
                P.wait("vector", s_mm, vm)
                P.wait("vector", s_xld, vr)
                P.wait("vector", s_st, max(0, P.val(s_st) - 2 * 16))
                slot = (pi * 2 + mb) % 2
                vv = v_op1(lambda E, bk=bk, slot=slot: E.tensor_tensor(
                    evac[:, slot * 512:(slot + 1) * 512], bank(bk), scrA[:, 0:512], ALU.add))
                ost["bankv"][bk] = vv
                P.wait("sync", s_ve, vv)
                y_dma(r_dram[row:row + 128, 3 + tb * 512: 3 + (tb + 1) * 512],
                      evac[:, slot * 512:(slot + 1) * 512], sem=s_st)
            ost["bufv"][b] = P.val(s_mm)
    drain_all()
    octx.close()
    tap_now("r1", r_dram)
    if upto == "p7":
        _finish(P, dbg, y_dma, out_d, r_dram, s_st, s_xld)
        with nc.Block() as block:
            P.emit(block)
        gctx.close()
        return nc, dbg

    fctx = ExitStack()
    F_ = fctx.enter_context
    xln_f = F_(nc.sbuf_tensor("xln_f", [128, NKD * T], BF16))
    fwbuf = [F_(nc.sbuf_tensor(f"fwbuf{i}", [128, NKD * 256], BF16)) for i in range(2)]
    laA = F_(nc.sbuf_tensor("laA", [128, NKD * 128], BF16))
    laB = F_(nc.sbuf_tensor("laB", [128, D], BF16))
    wrt = F_(nc.sbuf_tensor("wrt", [128, NKD * 8], BF16))
    hw_sb = F_(nc.sbuf_tensor("hw_sb", [128, T], BF16))
    wtm = F_(nc.sbuf_tensor("wtm", [128, 64], F32))
    wfm = F_(nc.sbuf_tensor("wfm", [8, T], F32))
    amask_sb = F_(nc.sbuf_tensor("amask_sb", [128, 4 * 512], F32))
    qh2 = [F_(nc.sbuf_tensor(f"qh{ii}", [128, 2 * T], BF16)) for ii in range(2)]
    kh2 = [F_(nc.sbuf_tensor(f"kh{ii}", [128, 2 * T], BF16)) for ii in range(2)]
    vh2 = [F_(nc.sbuf_tensor(f"vh{ii}", [128, 2 * T], BF16)) for ii in range(2)]
    ctxh = F_(nc.sbuf_tensor("ctxh", [128, 2 * T], BF16))
    pt_sb = F_(nc.sbuf_tensor("pt_sb", [128, 2 * 512], BF16))
    rrow = F_(nc.sbuf_tensor("rrow", [1, 512], F32))

    fst = {"buf": 0, "bufv": [0, 0], "bank": 0, "bankv": [0] * 8}

    def fload(wsrc, col0, width=256):
        b = fst["buf"]
        fst["buf"] ^= 1
        P.wait("gpsimd", s_mm, fst["bufv"][b])
        v = g_dma(fwbuf[b][:, 0:NKD * width].rearrange("p (c m) -> p c m", c=NKD),
                  wsrc[:, col0:col0 + width].rearrange("(c p) m -> p c m", p=128))
        return b, v

    def fsweep(wsrc, npieces, evac_fn, extra_mm=None):
        for pi in range(npieces):
            b, vld = fload(wsrc, pi * 256)
            P.wait("tensor", s_wld, vld)
            for mb in range(2):
                for tb in range(2):
                    bk = fst["bank"]
                    fst["bank"] = (fst["bank"] + 1) % 8
                    P.wait("tensor", s_ve, fst["bankv"][bk])
                    def f_mm2(E, b=b, mb=mb, tb=tb, bk=bk, pi=pi):
                        last = None
                        for k in range(NKD):
                            last = E.matmul(bank(bk),
                                            fwbuf[b][:, k * 256 + mb * 128: k * 256 + mb * 128 + 128],
                                            xln_f[:, k * T + tb * 512: k * T + tb * 512 + 512],
                                            start=(k == 0), stop=(extra_mm is None and k == NKD - 1))
                        if extra_mm is not None:
                            last = extra_mm(E, pi, mb, tb, bk)
                        return last
                    vm = t_op(f_mm2)
                    fst["bankv"][bk] = evac_fn(pi, mb, tb, bk, vm)
            fst["bufv"][b] = P.val(s_mm)

    def load_xln_f():
        P.wait("sync", s_mm, P.val(s_mm))
        P.wait("sync", s_ac, P.val(s_ac))
        v = y_dma(xln_f[:, :].rearrange("p (c t) -> p c t", c=NKD),
                  xln_dram[:, 3:TH].rearrange("(c p) t -> p c t", p=128))
        P.wait("tensor", s_xld, v)
        P.wait("vector", s_xld, v)
        return v

    def emit_ffn():
        g_dma(wrt[:, :].rearrange("p (c m) -> p c m", c=NKD),
              wrout_in[:, :].rearrange("(c p) m -> p c m", p=128))
        g_dma(laA[:, :].rearrange("p (c m) -> p c m", c=NKD),
              loraA_in[:, :].rearrange("(c p) m -> p c m", p=128))
        vw = g_dma(laB[:, :], loraB_in[:, :])
        P.wait("tensor", s_wld, vw)
        for tc in range(8):
            P.wait("tensor", s_ve, P.val(s_ve))
            P.wait("tensor", s_gp, P.val(s_gp))
            def f_rt(E, tc=tc):
                last = None
                for k in range(NKD):
                    last = E.matmul(bank(6)[:, 0:8],
                                    xln_f[:, k * T + tc * 128: k * T + tc * 128 + 128],
                                    wrt[:, k * 8:(k + 1) * 8],
                                    start=(k == 0), stop=(k == NKD - 1))
                return last
            vm = t_op(f_rt)
            P.wait("vector", s_mm, vm)
            vl = v_chain([
                lambda E: E.tensor_copy(scrB[:, 0:8], bank(6)[:, 0:8]),
                lambda E: E.reduce_max(colA[:, 0:1], bank(6)[:, 0:8], axis=AX, negate=True)])
            P.wait("scalar", s_ve, vl)
            ve_ = a_op1(lambda E: E.activation(scrB[:, 8:16], scrB[:, 0:8], AF.Exp,
                                               bias=colA[:, 0:1]))
            P.wait("vector", s_ac, ve_)
            vm1 = v_chain([
                lambda E: E.tensor_single_scalar(scrB[:, 16:24], scrB[:, 8:16], 1.0, ALU.is_ge),
                lambda E: E.tensor_sub(scrB[:, 24:32], scrB[:, 8:16], scrB[:, 16:24]),
                lambda E: E.reduce_max(colA[:, 1:2], scrB[:, 24:32], axis=AX)])
            P.wait("scalar", s_ve, vm1)
            vb = a_op1(lambda E: E.copy(colA[:, 2:3], colA[:, 1:2]))
            P.wait("vector", s_ac, vb)
            vsel = v_chain([
                lambda E: E.tensor_single_scalar(scrB[:, 32:40], scrB[:, 24:32], colA[:, 2:3], ALU.is_ge),
                lambda E: E.tensor_add(scrB[:, 32:40], scrB[:, 32:40], scrB[:, 16:24]),
                lambda E: E.tensor_mul(scrB[:, 40:48], scrB[:, 8:16], scrB[:, 32:40]),
                lambda E: E.reduce_sum(colA[:, 1:2], scrB[:, 40:48], axis=AX),
                lambda E: E.reciprocal(colA[:, 2:3], colA[:, 1:2])])
            P.wait("scalar", s_ve, vsel)
            vb2 = a_op1(lambda E: E.copy(colA[:, 3:4], colA[:, 2:3]))
            P.wait("vector", s_ac, vb2)
            vdv = v_op1(lambda E, tc=tc: E.tensor_single_scalar(
                wtm[:, tc * 8:(tc + 1) * 8], scrB[:, 40:48], colA[:, 3:4], ALU.mult))
            P.wait("tensor", s_ve, vdv)
            vm2 = t_op(lambda E, tc=tc: E.transpose(bank(7)[0:8, 0:128], wtm[:, tc * 8:(tc + 1) * 8],
                                                    ident_f[:, :]))
            P.wait("vector", s_mm, vm2)
            v_op1(lambda E, tc=tc: E.tensor_copy(wfm[0:8, tc * 128:(tc + 1) * 128],
                                                 bank(7)[0:8, 0:128]))
        if "wtm" in debug_taps and "dbg_wtm" not in dbg:
            o = nc.dram_tensor("dbg_wtm", [128, 64], F32, kind="ExternalOutput").ap()
            P.wait("sync", s_ve, P.val(s_ve))
            y_dma(o[:, :], wtm[:, :], sem=s_st)
            dbg["dbg_wtm"] = (o, None)
        P.wait("tensor", s_ve, P.val(s_ve))
        vm = t_op(lambda E: (E.matmul(bank(4), e16[:, :], wfm[0:8, 0:512], start=True, stop=True),
                             E.matmul(bank(5), e16[:, :], wfm[0:8, 512:1024],
                                      start=True, stop=True))[-1])
        P.wait("vector", s_mm, vm)
        v_op(lambda E: (E.tensor_copy(scrA[:, 0:512], bank(4)),
                        E.tensor_copy(scrA[:, 512:1024], bank(5))), n=1)
        for tb in range(2):
            P.wait("tensor", s_ve, P.val(s_ve))
            def f_h(E, tb=tb):
                last = None
                for k in range(NKD):
                    last = E.matmul(bank(tb), laA[:, k * 128:(k + 1) * 128],
                                    xln_f[:, k * T + tb * 512: k * T + tb * 512 + 512],
                                    start=(k == 0), stop=(k == NKD - 1))
                return last
            vm = t_op(f_h)
            P.wait("vector", s_mm, vm)
            v_op1(lambda E, tb=tb: E.tensor_tensor(hw_sb[:, tb * 512:(tb + 1) * 512],
                                                   bank(tb), scrA[:, tb * 512:(tb + 1) * 512],
                                                   ALU.mult))
        def extra_lora(E, pi, mb, tb, bk):
            return E.matmul(bank(bk), laB[:, pi * 256 + mb * 128: pi * 256 + mb * 128 + 128],
                            hw_sb[:, tb * 512:(tb + 1) * 512], start=False, stop=True)

        def evac_ffn(pi, mb, tb, bk, vm):
            row = pi * 256 + mb * 128
            ch = pi * 2 + mb
            P.wait("sync", s_ve, P.val(s_ve))
            vr = y_dma(scrA[:, 1536 + (tb % 2) * 512: 1536 + (tb % 2) * 512 + 512],
                       r_dram[row:row + 128, 3 + tb * 512: 3 + (tb + 1) * 512])
            P.wait("vector", s_mm, vm)
            P.wait("vector", s_xld, vr)
            P.wait("vector", s_st, max(0, P.val(s_st) - 2 * 16))
            slot = (pi * 2 + mb + tb) % 2
            vv = v_op1(lambda E, bk=bk, slot=slot, ch=ch, tb=tb: E.scalar_tensor_tensor(
                evac[:, slot * 512:(slot + 1) * 512], bank(bk), bbase[:, ch:ch + 1],
                scrA[:, 1536 + (tb % 2) * 512: 1536 + (tb % 2) * 512 + 512],
                ALU.add, ALU.add))
            P.wait("sync", s_ve, vv)
            y_dma(r_dram[row:row + 128, 3 + tb * 512: 3 + (tb + 1) * 512],
                  evac[:, slot * 512:(slot + 1) * 512], sem=s_st)
            return vv
        fsweep(wbase_in, 16, evac_ffn, extra_mm=extra_lora)

    emit_ln(r_dram, xln_dram, TH, c_start=3)
    drain_all()
    load_xln_f()
    emit_ffn()
    drain_all()
    tap_now("r2", r_dram)
    if upto == "p8":
        _finish(P, dbg, y_dma, out_d, r_dram, s_st, s_xld)
        with nc.Block() as block:
            P.emit(block)
        fctx.close()
        gctx.close()
        return nc, dbg

    emit_ln(r_dram, xln_dram, TH, c_start=3)
    drain_all()
    load_xln_f()
    g_dma(amask_sb[:, :], amask_in[:, :])

    def evac_q(pi, mb, tb, bk, vm):
        row = pi * 256 + mb * 128
        P.wait("vector", s_mm, vm)
        P.wait("vector", s_st, max(0, P.val(s_st) - 2 * 16))
        slot = (pi * 2 + mb + tb) % 2
        vv = v_op1(lambda E, bk=bk, slot=slot: E.tensor_copy(
            scrD[:, slot * 512:(slot + 1) * 512], bank(bk)))
        P.wait("sync", s_ve, vv)
        y_dma(q_cc[row:row + 128, tb * 512:(tb + 1) * 512],
              scrD[:, slot * 512:(slot + 1) * 512], sem=s_st)
        return vv
    fsweep(wq_in, 16, evac_q)

    for g in range(8):
        b, vld = fload(wkv_in, g * 256)
        P.wait("tensor", s_wld, vld)
        for tb in range(2):
            bk = fst["bank"]
            fst["bank"] = (fst["bank"] + 1) % 8
            P.wait("tensor", s_ve, fst["bankv"][bk])
            def f_k(E, b=b, tb=tb, bk=bk):
                last = None
                for k in range(NKD):
                    last = E.matmul(bank(bk), fwbuf[b][:, k * 256: k * 256 + 128],
                                    xln_f[:, k * T + tb * 512: k * T + tb * 512 + 512],
                                    start=(k == 0), stop=(k == NKD - 1))
                return last
            vm = t_op(f_k)
            P.wait("vector", s_mm, vm)
            P.wait("vector", s_st, max(0, P.val(s_st) - 2 * 16))
            slot = tb % 2
            vv = v_op1(lambda E, bk=bk, slot=slot: E.tensor_copy(
                scrD[:, slot * 512:(slot + 1) * 512], bank(bk)))
            fst["bankv"][bk] = vv
            P.wait("sync", s_ve, vv)
            y_dma(k_cc[g * 128:(g + 1) * 128, tb * 512:(tb + 1) * 512],
                  scrD[:, slot * 512:(slot + 1) * 512], sem=s_st)
        for tc in range(8):
            bk = fst["bank"]
            fst["bank"] = (fst["bank"] + 1) % 8
            P.wait("tensor", s_ve, fst["bankv"][bk])
            def f_v(E, b=b, tc=tc, bk=bk):
                last = None
                for k in range(NKD):
                    last = E.matmul(bank(bk)[:, 0:128],
                                    xln_f[:, k * T + tc * 128: k * T + tc * 128 + 128],
                                    fwbuf[b][:, k * 256 + 128: k * 256 + 256],
                                    start=(k == 0), stop=(k == NKD - 1))
                return last
            vm = t_op(f_v)
            P.wait("vector", s_mm, vm)
            P.wait("vector", s_st, max(0, P.val(s_st) - 2 * 16))
            slot = tc % 2
            vv = v_op1(lambda E, bk=bk, slot=slot: E.tensor_copy(
                scrD[:, slot * 512: slot * 512 + 128], bank(bk)[:, 0:128]))
            fst["bankv"][bk] = vv
            P.wait("sync", s_ve, vv)
            y_dma(v_cc[tc * 128:(tc + 1) * 128, g * 128:(g + 1) * 128],
                  scrD[:, slot * 512: slot * 512 + 128], sem=s_st)
        fst["bufv"][b] = P.val(s_mm)
    drain_all()

    vcc = P.bump(s_cc)
    P.em("gpsimd", lambda E: E.collective_compute(
        "AllGather", ALU.bypass,
        replica_groups=[[0, 1], [2, 3], [4, 5], [6, 7]],
        ins=[q_cc[:, :]], outs=[q_g[:, :]]).then_inc(s_cc, 1))
    for src_t, dst_t in ((k_cc, k_g), (v_cc, v_g)):
        vcc = P.bump(s_cc)
        P.em("gpsimd", lambda E, s=src_t, d=dst_t: E.collective_compute(
            "AllGather", ALU.bypass,
            replica_groups=[[0, 1, 2, 3, 4, 5, 6, 7]],
            ins=[s[:, :]], outs=[d[:, :]]).then_inc(s_cc, 1))
    P.wait("gpsimd", s_cc, P.val(s_cc))

    _regctr = [0]

    def dyn_dma(dst, srcfn, idx_col, scale, extra=0, maxv=2048):
        v = P.bump(s_wld, 16)
        _regctr[0] += 1
        _rn = f"rdy{_regctr[0]}"
        def f(E, d=dst, sf=srcfn, _rn=_rn, maxv=maxv):
            with E.register(_rn) as r:
                E.reg_load(r, idxs[0:1, idx_col:idx_col + 1])
                rv = RuntimeValue(val=r, min_val=0, max_val=maxv)
                E.dma_start(out=d, in_=sf(rv * scale + extra)).then_inc(s_wld, 16)
        P.em("gpsimd", f)
        return v

    for r in range(2):
        dyn_dma(q_gg[r * 2048:(r + 1) * 2048, :],
                lambda off, r=r: q_g[bass.ds(off, 2048), :], 3, 1, r * D, maxv=2048)
    for slot in range(8):
        dyn_dma(k_gg[slot * 128:(slot + 1) * 128, :],
                lambda off, slot=slot: k_g[bass.ds(off, 128), :], 5, 1, slot * AG * 128,
                maxv=896)
    vq = dyn_dma(v_gg[:, :], lambda off: v_g[:, bass.ds(off, 128)], 5, 1, 0, maxv=896)
    P.wait("sync", s_wld, P.val(s_wld))

    def head_loads(i):
        bb = i // 4
        bufp = i % 2
        vq = 0
        for r in range(2):
            slot = 2 * bb + r
            vq = y_dma(qh2[bufp][:, r * T:(r + 1) * T],
                       q_gg[r * 2048 + i * 128: r * 2048 + (i + 1) * 128, :])
            vq = y_dma(kh2[bufp][:, r * T:(r + 1) * T], k_gg[slot * 128:(slot + 1) * 128, :])
            vq = y_dma(vh2[bufp][:, r * T:(r + 1) * T].rearrange("p (tc q) -> p tc q", tc=8),
                       v_gg[slot * T:(slot + 1) * T, :]
                       .rearrange("(tc p) hd -> p tc hd", p=128))
        return vq

    vq_l = {0: head_loads(0)}
    for i in range(16):
        qh, kh, vh = qh2[i % 2], kh2[i % 2], vh2[i % 2]
        if i + 1 < 16:
            P.wait("sync", s_mm, P.val(s_mm))
            vq_l[i + 1] = head_loads(i + 1)
        vq = vq_l[i]
        P.wait("tensor", s_xld, vq)
        P.wait("scalar", s_xld, vq)
        P.wait("scalar", s_ve, P.val(s_ve))
        va = a_op1(lambda E, qh=qh: E.activation(scrD[:, 0:2048], qh[:, :], AF.Square))
        va = a_op1(lambda E, kh=kh: E.activation(scrD[:, 2048:3072], kh[:, 0:1024], AF.Square))
        va = a_op1(lambda E, kh=kh: E.activation(scrC[:, 0:1024], kh[:, 1024:2048], AF.Square))
        P.wait("tensor", s_ac, va)
        def f_nrm2(E):
            last = None
            for j in range(4):
                last = E.matmul(pp[0:1, j * 512:(j + 1) * 512], ones_col_b[:, 0:1],
                                scrD[:, j * 512:(j + 1) * 512], start=True, stop=True)
            for j in range(2):
                last = E.matmul(pp[0:1, 2048 + j * 512: 2048 + (j + 1) * 512],
                                ones_col_b[:, 0:1], scrD[:, 2048 + j * 512: 2048 + (j + 1) * 512],
                                start=True, stop=True)
            for j in range(2):
                last = E.matmul(pp[0:1, 3072 + j * 512: 3072 + (j + 1) * 512],
                                ones_col_b[:, 0:1], scrC[:, j * 512:(j + 1) * 512],
                                start=True, stop=True)
            return last
        vm = t_op(f_nrm2)
        P.wait("vector", s_mm, vm)
        vv = v_chain([
            lambda E: E.reduce_max(rowA[0:1, 0:1], pp[0:1, 0:2048], axis=AX),
            lambda E: E.reduce_max(rowA[0:1, 1:2], pp[0:1, 2048:4096], axis=AX)])
        P.wait("scalar", s_ve, vv)
        va = a_op1(lambda E: E.activation(rowA[0:1, 2:3], rowA[0:1, 0:1], AF.Ln))
        va = a_op1(lambda E: E.activation(rowA[0:1, 3:4], rowA[0:1, 1:2], AF.Ln))
        P.wait("vector", s_ac, va)
        vv = v_chain([
            lambda E: E.tensor_add(rowA[0:1, 4:5], rowA[0:1, 2:3], rowA[0:1, 3:4]),
            lambda E: E.tensor_scalar_add(rowA[0:1, 4:5], rowA[0:1, 4:5], 2.0 * math.log(ASCALE))])
        P.wait("scalar", s_ve, vv)
        va = a_op1(lambda E: E.activation(rowA[0:1, 5:6], rowA[0:1, 4:5], AF.Exp, scale=0.5))
        P.wait("vector", s_ac, va)
        vv = v_op1(lambda E: E.tensor_scalar_mul(rowA[0:1, 6:7], rowA[0:1, 5:6], -1.0))
        P.wait("tensor", s_ve, vv)
        vm = t_op(lambda E: E.matmul(bank(6)[:, 0:1], ones_row[0:1, :], rowA[0:1, 6:7],
                                     start=True, stop=True))
        P.wait("vector", s_mm, vm)
        vneg = v_op1(lambda E: E.tensor_copy(colB[:, 0:1] if False else colA[:, 0:1], bank(6)[:, 0:1]))

        for bnd in range(4):
            nki = 4 * (bnd + 1)
            for ki in range(nki):
                sbk = ki % 2
                P.wait("tensor", s_ve, P.val(s_ve)) if ki < 2 else None
                def f_st(E, ki=ki, bnd=bnd, sbk=sbk, kh=kh, qh=qh):
                    return E.matmul(bank(sbk), kh[:, ki * 128:(ki + 1) * 128],
                                    qh[:, bnd * 512:(bnd + 1) * 512], start=True, stop=True)
                vm = t_op(f_st)
                rel_ = ki - 4 * bnd
                if rel_ >= 0:
                    P.wait("vector", s_mm, vm)
                    vv = v_op1(lambda E, sbk=sbk, rel_=rel_: E.tensor_tensor(
                        scrA[:, 0:512], bank(sbk), amask_sb[:, rel_ * 512:(rel_ + 1) * 512],
                        ALU.add))
                    P.wait("scalar", s_ve, vv)
                    va = a_op1(lambda E, ki=ki: E.activation(
                        pt_sb[:, (ki % 2) * 512:(ki % 2) * 512 + 512], scrA[:, 0:512], AF.Exp,
                        scale=ASCALE, bias=colA[:, 0:1]))
                else:
                    P.wait("scalar", s_mm, vm)
                    va = a_op1(lambda E, ki=ki, sbk=sbk: E.activation(
                        pt_sb[:, (ki % 2) * 512:(ki % 2) * 512 + 512], bank(sbk), AF.Exp,
                        scale=ASCALE, bias=colA[:, 0:1]))
                P.wait("tensor", s_ac, va)
                def f_acc(E, ki=ki, nki=nki, vh=vh):
                    E.matmul(bank(6)[0:1, 0:512], ones_col_b[:, 0:1],
                             pt_sb[:, (ki % 2) * 512:(ki % 2) * 512 + 512],
                             start=(ki == 0), stop=(ki == nki - 1))
                    return E.matmul(pp[:, 2048:2560], vh[:, ki * 128:(ki + 1) * 128],
                                    pt_sb[:, (ki % 2) * 512:(ki % 2) * 512 + 512],
                                    start=(ki == 0), stop=(ki == nki - 1))
                vm = t_op(f_acc)
            P.wait("vector", s_mm, vm)
            vv = v_op1(lambda E: E.reciprocal(rrow[0:1, 0:512], bank(6)[0:1, 0:512]))
            P.wait("tensor", s_ve, vv)
            vm2 = t_op(lambda E: E.matmul(bank(7), ones_row[0:1, :], rrow[0:1, 0:512],
                                          start=True, stop=True))
            P.wait("vector", s_mm, vm2)
            vv2 = v_op(lambda E, bnd=bnd: (
                E.tensor_copy(scrA[:, 512:1024], bank(7)),
                E.tensor_tensor(ctxh[:, bnd * 512:(bnd + 1) * 512], pp[:, 2048:2560],
                                scrA[:, 512:1024], ALU.mult)), n=2)
        P.wait("sync", s_ve, P.val(s_ve))
        y_dma(ctx_dram[i * 128:(i + 1) * 128, :], ctxh[:, :], sem=s_st)
    P.wait("gpsimd", s_st, P.val(s_st))
    vdy = P.bump(s_gst, 16)
    def f_c3(E):
        with E.register("rc3") as r:
            E.reg_load(r, idxs[0:1, 6:7])
            rv = RuntimeValue(val=r, min_val=0, max_val=1024)
            E.dma_start(out=cc3_in[:, :],
                        in_=ctx_dram[:, bass.ds(rv, 1024)]).then_inc(s_gst, 16)
    P.em("gpsimd", f_c3)
    drain_all()

    vcc = P.bump(s_cc)
    P.em("gpsimd", lambda E: E.collective_compute(
        "AllGather", ALU.bypass,
        replica_groups=[[0, 1], [2, 3], [4, 5], [6, 7]],
        ins=[cc3_in[:, :]], outs=[cc3_out[:, :]]).then_inc(s_cc, 1))
    P.wait("gpsimd", s_cc, vcc)
    v1 = P.bump(s_wld, 16)
    def f_as1(E):
        with E.register("ra1") as r:
            E.reg_load(r, idxs[0:1, 3:4])
            rv = RuntimeValue(val=r, min_val=0, max_val=2048)
            with E.register("ra2") as r2:
                E.reg_load(r2, idxs[0:1, 7:8])
                rv2 = RuntimeValue(val=r2, min_val=0, max_val=1024)
                E.dma_start(out=ctx_all[bass.ds(rv, 2048), :],
                            in_=ctx_dram[:, bass.ds(rv2, 1024)]).then_inc(s_wld, 16)
    P.em("gpsimd", f_as1)
    v1 = P.bump(s_wld, 16)
    def f_as2(E):
        with E.register("ra3") as r:
            E.reg_load(r, idxs[0:1, 4:5])
            rv = RuntimeValue(val=r, min_val=0, max_val=2048)
            with E.register("ra4") as r2:
                E.reg_load(r2, idxs[0:1, 0:1])
                rv2 = RuntimeValue(val=r2, min_val=0, max_val=1)
                E.dma_start(out=ctx_all[bass.ds(rv, 2048), :],
                            in_=cc3_out[bass.ds(rv2 * 2048, 2048), :]).then_inc(s_wld, 16)
    P.em("gpsimd", f_as2)
    drain_all()

    P.wait("sync", s_mm, P.val(s_mm))
    vctx = y_dma(xln_f[:, :].rearrange("p (c t) -> p c t", c=NKD),
                 ctx_all[:, :].rearrange("(c p) t -> p c t", p=128))
    P.wait("tensor", s_xld, vctx)

    def evac_wo(pi, mb, tb, bk, vm):
        row = pi * 256 + mb * 128
        ch = pi * 2 + mb
        P.wait("sync", s_ve, P.val(s_ve))
        vr = y_dma(scrA[:, 1536 + (tb % 2) * 512: 1536 + (tb % 2) * 512 + 512],
                   r_dram[row:row + 128, 3 + tb * 512: 3 + (tb + 1) * 512])
        P.wait("vector", s_mm, vm)
        P.wait("vector", s_xld, vr)
        P.wait("vector", s_st, max(0, P.val(s_st) - 2 * 16))
        slot = (pi * 2 + mb + tb) % 2
        vv = v_op1(lambda E, bk=bk, slot=slot, ch=ch, tb=tb: E.scalar_tensor_tensor(
            evac[:, slot * 512:(slot + 1) * 512], bank(bk), bocol[:, ch:ch + 1],
            scrA[:, 1536 + (tb % 2) * 512: 1536 + (tb % 2) * 512 + 512],
            ALU.add, ALU.add))
        P.wait("sync", s_ve, vv)
        y_dma(r_dram[row:row + 128, 3 + tb * 512: 3 + (tb + 1) * 512],
              evac[:, slot * 512:(slot + 1) * 512], sem=s_st)
        return vv
    fsweep(wo_in, 16, evac_wo)
    drain_all()
    tap_now("r3", r_dram)
    if upto == "p14":
        _finish(P, dbg, y_dma, out_d, r_dram, s_st, s_xld)
        with nc.Block() as block:
            P.emit(block)
        fctx.close()
        gctx.close()
        return nc, dbg

    emit_ln(r_dram, xln_dram, TH, c_start=3)
    drain_all()
    load_xln_f()
    emit_ffn()
    drain_all()
    fctx.close()

    _finish(P, dbg, y_dma, out_d, r_dram, s_st, s_xld)
    with nc.Block() as block:
        P.emit(block)
    gctx.close()
    return nc, dbg


def _finish(P, dbg, y_dma, out_d, r_dram, s_st, s_xld):
    for name, (o, apx) in dbg.items():
        if apx is not None:
            y_dma(o[:, :], apx[:, :], sem=s_st)
    y_dma(out_d[:, :], r_dram[:, 3:TH], sem=s_st)
    P.wait("sync", s_st, P.val(s_st))
    P.wait("sync", s_xld, P.val(s_xld))



def _mk_inputs(inputs):
    x = np.asarray(inputs["L0_attn_a_hidden_states"], dtype=np.float32)
    wip = np.ascontiguousarray(np.asarray(inputs["L0_attn_a_in_proj_w"], np.float32).T)
    convw = np.asarray(inputs["L0_attn_a_conv_w"], np.float32).reshape(CONV_DIM, 4)
    convw = np.ascontiguousarray(convw.reshape(80, 128, 4).transpose(1, 0, 2).reshape(128, 320))
    convb = np.ascontiguousarray(np.asarray(inputs["L0_attn_a_conv_b"], np.float32)
                                 .reshape(80, 128).T)
    dtb = np.asarray(inputs["L0_attn_a_dt_bias"], np.float32).reshape(128, 1)
    Dv = np.asarray(inputs["L0_attn_a_D"], np.float32).reshape(1, 128)
    normw = np.asarray(inputs["L0_attn_a_norm_w"], np.float32).reshape(1, INTER)
    wop = np.ascontiguousarray(np.asarray(inputs["L0_attn_a_out_proj_w"], np.float32).T)
    wq = np.ascontiguousarray(np.asarray(inputs["L0_attn_b_Wq"], np.float32))
    wkv = np.ascontiguousarray(np.asarray(inputs["L0_attn_b_Wkv"], np.float32))
    wo = np.ascontiguousarray(np.asarray(inputs["L0_attn_b_Wo"], np.float32))
    bo = np.ascontiguousarray(np.asarray(inputs["L0_attn_b_bo"], np.float32).reshape(NKD, 128).T)
    wbase = np.ascontiguousarray(np.asarray(inputs["L0_ffn_W_base"], np.float32).T)
    bbase = np.ascontiguousarray(np.asarray(inputs["L0_ffn_b_base"], np.float32).reshape(NKD, 128).T)
    wrout = np.ascontiguousarray(np.asarray(inputs["L0_ffn_W_router"], np.float32).T)
    loraA = np.ascontiguousarray(np.asarray(inputs["L0_ffn_lora_A"], np.float32)
                                 .transpose(2, 0, 1).reshape(D, 128))
    loraB = np.ascontiguousarray(np.asarray(inputs["L0_ffn_lora_B"], np.float32)
                                 .transpose(0, 2, 1).reshape(128, D))

    acol = -(np.arange(1, 129, dtype=np.float32)).reshape(128, 1)
    e16v = np.zeros((E_EXP, 128), np.float32)
    for e in range(E_EXP):
        e16v[e, e * R_LORA:(e + 1) * R_LORA] = LORA_SCALE
    ones_row = np.ones((1, 128), np.float32)
    ident = np.eye(128, dtype=np.float32)
    sidx = np.arange(128).reshape(128, 1)
    lidx = np.arange(128).reshape(1, 128)
    tril = (lidx >= sidx).astype(np.float32)
    am = np.zeros((128, 4, 512), np.float32)
    for r in range(4):
        kk = np.arange(128).reshape(128, 1)
        qq = np.arange(512).reshape(1, 512)
        am[:, r, :] = np.where(r * 128 + kk > qq, -1e9, 0.0)
    am = am.reshape(128, 4 * 512)

    shared = {
        "wt_inproj": wip, "convw": convw, "convb": convb, "dtbias_col": dtb,
        "a_col": acol, "d_row": Dv, "normw_row": normw, "wt_outproj": wop,
        "wq": wq, "wkv": wkv, "wo": wo, "bo_col": bo, "wt_base": wbase,
        "bbase_col": bbase, "wt_router": wrout, "loraA_t": loraA, "loraB_t": loraB,
        "e16": e16v, "ones_row": ones_row, "ident_f": ident, "tril_sl": tril,
        "attn_masks": np.ascontiguousarray(am),
    }
    ims = []
    for c in range(8):
        s, r = c // 2, c % 2
        xs = np.ascontiguousarray(x[s, r * T:(r + 1) * T, :].T)
        halo = (np.ascontiguousarray(x[s, r * T - 3:r * T, :].T) if r == 1
                else np.zeros((D, 3), np.float32))
        hbase = 0 if r == 0 else 16
        idx = np.array([[1 - r, r, hbase, hbase * 128, (16 - hbase) * 128,
                         (2 * s + hbase // 16) * 128, (1 - r) * 1024, r * 1024]], np.int32)
        gate = np.full((128, 1), float(r), np.float32)
        im = dict(shared)
        im.update({"x_fm": xs, "xhalo_fm": halo, "gate_col": gate, "idx": idx})
        ims.append(im)
    return ims


def kernel(**inputs):
    ims = _mk_inputs(inputs)
    nc, _ = build_program()
    res = run_bass_kernel_spmd(nc, ims, core_ids=list(range(8)))
    full = np.zeros((4, 2048, D), np.float32)
    for c in range(8):
        s, r = c // 2, c % 2
        full[s, r * T:(r + 1) * T, :] = np.asarray(res.results[c]["out_fm"]).T
    return full
